# version 23 (final)
# baseline (speedup 1.0000x reference)
import numpy as np
from contextlib import ExitStack

import concourse.bass as bass
import concourse.mybir as mybir
from concourse.bass_utils import run_bass_kernel_spmd

F32 = mybir.dt.float32
BF16 = mybir.dt.bfloat16
I32 = mybir.dt.int32
AF = mybir.ActivationFunctionType
ALU = mybir.AluOpType
AX = mybir.AxisListType.X

DM = 1024
S_FULL = 4096
HALF = 2048
NB = 4
NC = 8
H0, KV0, D0, RD = 8, 2, 256, 64
E0, I0 = 8, 2048
H1, D1, CH = 16, 64, 128
E1 = 8
DC = DM // 128
NT = HALF // 128
LNR = 1.0 / DM
DBG = False


class Sem:
    def __init__(self, nc, name):
        self.h = nc.alloc_semaphore(name)
        self.n = 0

    def inc(self, instr, k=1):
        instr.then_inc(self.h, k)
        self.n += k
        return self.n


class Cx:
    def __init__(self, nc, es):
        self.nc = nc
        self.es = es
        self.dm = Sem(nc, "dm")
        self.gd = Sem(nc, "gd")
        self.pe = Sem(nc, "pe")
        self.act = Sem(nc, "act")
        self.dve = Sem(nc, "dve")
        self.aux = Sem(nc, "aux")
        self.hsp = Sem(nc, "hsp")

    def sb(self, shape, dtype, es=None, side=None):
        self.nsb = getattr(self, "nsb", 0) + 1
        return (es or self.es).enter_context(
            self.nc.sbuf_tensor(f"sb{self.nsb}", shape, dtype, side=side))



def emit_identity_ones(nc, P, cx):
    with nc.Block() as blk:
        @blk.gpsimd
        def _(g):
            g.memset(P["idf"][:], 0.0)
            m = cx.aux.inc(g.memset(P["idb"][:], 0.0))
            g.wait_ge(cx.aux.h, m)
            g.affine_select(P["idf"][:], P["idf"][:], pattern=[[-1, 128]],
                            compare_op=ALU.not_equal, fill=1.0, base=0,
                            channel_multiplier=1)
            g.affine_select(P["idb"][:], P["idb"][:], pattern=[[-1, 128]],
                            compare_op=ALU.not_equal, fill=1.0, base=0,
                            channel_multiplier=1)
        @blk.vector
        def _(v):
            v.memset(P["ones_row"][:], 1.0)
            v.memset(P["ones_col"][:], 1.0)


def emit_ln_blocks(cx, x_tiles, xn_tiles, sq, stw, ntiles):
    nc = cx.nc
    with nc.Block() as blk:
        @blk.vector
        def _(v):
            for i in range(ntiles):
                stv = stw[:, i * 8:(i + 1) * 8]
                v.reduce_sum(stv[:, 0:1], x_tiles(i), axis=AX)
                v.tensor_mul(sq[:], x_tiles(i), x_tiles(i))
                v.reduce_sum(stv[:, 1:2], sq[:], axis=AX)
                m = cx.dve.inc(v.tensor_scalar_mul(stv[:, 2:3], stv[:, 0:1], LNR))
                v.wait_ge(cx.dve.h, m)
                m = cx.dve.inc(v.tensor_scalar(stv[:, 3:4], stv[:, 2:3], stv[:, 2:3],
                                               1e-5, op0=ALU.mult, op1=ALU.subtract))
                v.wait_ge(cx.dve.h, m)
                v.tensor_scalar(stv[:, 4:5], stv[:, 1:2], LNR, stv[:, 3:4],
                                op0=ALU.mult, op1=ALU.subtract)
    with nc.Block() as blk:
        @blk.scalar
        def _(s):
            for i in range(ntiles):
                stv = stw[:, i * 8:(i + 1) * 8]
                s.activation(stv[:, 5:6], stv[:, 4:5], AF.Ln)
                s.activation(stv[:, 6:7], stv[:, 5:6], AF.Exp, scale=-0.5)
    with nc.Block() as blk:
        @blk.vector
        def _(v):
            for i in range(ntiles):
                stv = stw[:, i * 8:(i + 1) * 8]
                v.tensor_scalar(xn_tiles(i), x_tiles(i), stv[:, 2:3], stv[:, 6:7],
                                op0=ALU.subtract, op1=ALU.mult)


def emit_transpose_block(cx, P, pst, jobs, ident_key="idb"):
    nc = cx.nc
    groups = [jobs[i:i + 4] for i in range(0, len(jobs), 4)]
    with nc.Block() as blk:
        pe0, dv0 = cx.pe.n, cx.dve.n
        @blk.tensor
        def _(te):
            done = 0
            for g, grp in enumerate(groups):
                if g >= 2:
                    te.wait_ge(cx.dve.h, dv0 + sum(len(x) for x in groups[:g - 1]))
                bk = pst[g % 2]
                for j, (src, _dst) in enumerate(grp):
                    cx.pe.inc(te.transpose(bk[:, j * 128:(j + 1) * 128], src,
                                           P[ident_key][:]))
                done += len(grp)
        @blk.vector
        def _(v):
            done = 0
            for g, grp in enumerate(groups):
                done += len(grp)
                v.wait_ge(cx.pe.h, pe0 + done)
                bk = pst[g % 2]
                for j, (_src, dst) in enumerate(grp):
                    cx.dve.inc(v.tensor_copy(dst, bk[:, j * 128:(j + 1) * 128]))


def emit_rope_block(cx, ps_list, groups, cs_view, sn_view, rr, pass_scale):
    raise NotImplementedError



def emit_l0(cx, t, P, ps, pst):
    nc = cx.nc

    aoes = ExitStack()
    ao = None

    def aosl(r, q5):
        return ao[:, r * HALF + q5 * 512:r * HALF + (q5 + 1) * 512]

    with ExitStack() as kves:
        kfm = cx.sb([128, 4 * S_FULL], BF16, kves)
        vtm = cx.sb([128, 32 * 512], BF16, kves)
        cq = cx.sb([64, 2 * HALF], F32, kves)

        with ExitStack() as ph:
            cs = cx.sb([64, 2 * 1024], F32, ph)
            xin = cx.sb([128, 8 * DM], F32, ph)
            xnt = cx.sb([128, 4 * DM], BF16, ph)
            xf = cx.sb([128, DC * 512], BF16, ph)
            sq = cx.sb([128, DM], F32, ph)
            stw = cx.sb([128, 4 * 8], F32, ph)
            wkv = cx.sb([128, 2 * 4096], BF16, ph)
            rr = cx.sb([64, 3 * 512], F32, ph)

            x_mk = {}
            def x_load(sy, s5_):
                b = (s5_ % 2) * 4 * DM
                for i in range(4):
                    cx.dm.inc(sy.dma_start(
                        xin[:, b + i * DM:b + (i + 1) * DM],
                        t["x0"][s5_ * 512 + i * 128:s5_ * 512 + (i + 1) * 128, :]), 16)
                cb = (s5_ % 2) * 1024
                cx.dm.inc(sy.dma_start(cs[:, cb:cb + 512],
                                       t["cosk"][:, s5_ * 512:(s5_ + 1) * 512]), 16)
                cx.dm.inc(sy.dma_start(cs[:, cb + 512:cb + 1024],
                                       t["sink"][:, s5_ * 512:(s5_ + 1) * 512]), 16)
                x_mk[s5_] = cx.dm.n

            with nc.Block() as blk:
                @blk.gpsimd
                def _(g):
                    for c in range(DC):
                        cx.gd.inc(g.dma_start(wkv[:, c * 512:(c + 1) * 512],
                                              t["wk0"][c * 128:(c + 1) * 128, :]), 16)
                        cx.gd.inc(g.dma_start(wkv[:, 4096 + c * 512:4096 + (c + 1) * 512],
                                              t["wv0"][c * 128:(c + 1) * 128, :]), 16)
                    g.wait_ge(cx.gd.h, cx.gd.n)
                @blk.sync
                def _(sy):
                    cx.dm.inc(sy.dma_start(cq[:, 0:HALF], t["cosq"]), 16)
                    cx.dm.inc(sy.dma_start(cq[:, HALF:], t["sinq"]), 16)
                    mcq = cx.dm.n
                    x_load(sy, 0)
                    sy.wait_ge(cx.dm.h, mcq)

            for s5 in range(8):
                own = s5 < 4
                with nc.Block() as blk:
                    @blk.sync
                    def _(sy):
                        sy.wait_ge(cx.dm.h, x_mk[s5])
                        if s5 + 1 < 8:
                            x_load(sy, s5 + 1)
                xb = (s5 % 2) * 4 * DM
                emit_ln_blocks(cx, lambda i: xin[:, xb + i * DM:xb + (i + 1) * DM],
                               lambda i: xnt[:, i * DM:(i + 1) * DM], sq, stw, 4)
                jobs = []
                for i in range(4):
                    for c in range(DC):
                        src = xnt[:, i * DM + c * 128:i * DM + (c + 1) * 128]
                        if own:
                            dst = P["xn"][:, c * HALF + s5 * 512 + i * 128:
                                          c * HALF + s5 * 512 + (i + 1) * 128]
                        else:
                            dst = xf[:, c * 512 + i * 128:c * 512 + (i + 1) * 128]
                        jobs.append((src, dst))
                emit_transpose_block(cx, P, pst, jobs)

                if own:
                    def xc(c, s5=s5):
                        return P["xn"][:, c * HALF + s5 * 512:c * HALF + (s5 + 1) * 512]
                else:
                    def xc(c):
                        return xf[:, c * 512:(c + 1) * 512]

                with nc.Block() as blk:
                    pe0, ac0, dv0 = cx.pe.n, cx.act.n, cx.dve.n
                    @blk.tensor
                    def _(te):
                        for dc in range(4):
                            mm = None
                            for c in range(DC):
                                mm = te.matmul(
                                    ps[dc][:],
                                    wkv[:, c * 512 + dc * 128:c * 512 + (dc + 1) * 128],
                                    xc(c), start=(c == 0), stop=(c == DC - 1))
                            cx.pe.inc(mm)
                    @blk.scalar
                    def _(s):
                        nrope = 0
                        for dc in range(4):
                            s.wait_ge(cx.pe.h, pe0 + dc + 1)
                            kd = kfm[:, dc * S_FULL + s5 * 512:dc * S_FULL + (s5 + 1) * 512]
                            if dc % 2 == 0:
                                if nrope >= 1:
                                    s.wait_ge(cx.dve.h, dv0 + nrope)
                                s.activation(rr[0:32, 0:512], ps[dc][32:64, :],
                                             AF.Copy, scale=-1.0)
                                cx.act.inc(s.activation(rr[32:64, 0:512],
                                                        ps[dc][0:32, :], AF.Copy))
                                s.activation(kd[64:128, :], ps[dc][64:128, :], AF.Copy)
                                nrope += 1
                            else:
                                s.activation(kd[0:64, :], ps[dc][0:64, :], AF.Copy)
                                s.activation(kd[64:128, :], ps[dc][64:128, :], AF.Copy)
                    @blk.vector
                    def _(v):
                        cb = (s5 % 2) * 1024
                        csv = cs[:, cb:cb + 512]
                        snv = cs[:, cb + 512:cb + 1024]
                        for j, dc in enumerate([0, 2]):
                            v.wait_ge(cx.act.h, ac0 + j + 1)
                            kd = kfm[:, dc * S_FULL + s5 * 512:dc * S_FULL + (s5 + 1) * 512]
                            v.tensor_mul(rr[:, 512:1024], ps[dc][0:64, :], csv)
                            v.tensor_mul(rr[:, 1024:1536], rr[0:64, 0:512], snv)
                            cx.dve.inc(v.tensor_add(kd[0:64, :], rr[:, 512:1024],
                                                    rr[:, 1024:1536]))

                with nc.Block() as blk:
                    pe0 = cx.pe.n
                    @blk.tensor
                    def _(te):
                        for i in range(4):
                            mm = None
                            for c in range(DC):
                                mm = te.matmul(
                                    ps[i][:], xc(c)[:, i * 128:(i + 1) * 128],
                                    wkv[:, 4096 + c * 512:4096 + (c + 1) * 512],
                                    start=(c == 0), stop=(c == DC - 1))
                            cx.pe.inc(mm)
                    @blk.scalar
                    def _(s):
                        for i in range(4):
                            s.wait_ge(cx.pe.h, pe0 + i + 1)
                            r = s5 * 4 + i
                            s.activation(vtm[:, r * 512:(r + 1) * 512], ps[i][:], AF.Copy)

        if DBG:
            with nc.Block() as blk:
                @blk.gpsimd
                def _(g):
                    cx.gd.inc(g.dma_start(P["dbg_xn"], P["xn"][:]), 16)
                    cx.gd.inc(g.dma_start(P["dbg_kfm"], kfm[:]), 16)
                    cx.gd.inc(g.dma_start(P["dbg_vtm"], vtm[:]), 16)
                    g.wait_ge(cx.gd.h, cx.gd.n)
        ao = cx.sb([128, 16 * HALF], BF16, aoes, side="right")
        with ExitStack() as ph:
            qw = cx.sb([128, 8 * 256], BF16, ph)
            qh = cx.sb([128, 2 * HALF], BF16, ph)
            rr = cx.sb([64, 3 * 512], F32, ph)
            expt = cx.sb([128, 4 * 512], BF16, ph)
            rden = cx.sb([1, 512], F32, ph)
            rdbc = cx.sb([128, 512], F32, ph)

            for h in range(H0):
                kv = h // 4
                with nc.Block() as blk:
                    @blk.gpsimd
                    def _(g):
                        for c in range(DC):
                            cx.gd.inc(g.dma_start(
                                qw[:, c * 256:(c + 1) * 256],
                                t["wq0"][c * 128:(c + 1) * 128, h * 256:(h + 1) * 256]), 16)
                        g.wait_ge(cx.gd.h, cx.gd.n)
                for oc in range(2):
                    with nc.Block() as blk:
                        pe0, ac0, dv0 = cx.pe.n, cx.act.n, cx.dve.n
                        @blk.tensor
                        def _(te):
                            for q5 in range(4):
                                mm = None
                                for c in range(DC):
                                    mm = te.matmul(
                                        ps[q5][:],
                                        qw[:, c * 256 + oc * 128:c * 256 + (oc + 1) * 128],
                                        P["xn"][:, c * HALF + q5 * 512:c * HALF + (q5 + 1) * 512],
                                        start=(c == 0), stop=(c == DC - 1))
                                cx.pe.inc(mm)
                        @blk.scalar
                        def _(s):
                            for q5 in range(4):
                                s.wait_ge(cx.pe.h, pe0 + q5 + 1)
                                qd = qh[:, oc * HALF + q5 * 512:oc * HALF + (q5 + 1) * 512]
                                if oc == 0:
                                    if q5 >= 1:
                                        s.wait_ge(cx.dve.h, dv0 + q5)
                                    s.activation(rr[0:32, 0:512], ps[q5][32:64, :],
                                                 AF.Copy, scale=-1.0)
                                    cx.act.inc(s.activation(rr[32:64, 0:512],
                                                            ps[q5][0:32, :], AF.Copy))
                                    s.activation(qd[64:128, :], ps[q5][64:128, :],
                                                 AF.Copy, scale=0.0625)
                                else:
                                    s.activation(qd[0:64, :], ps[q5][0:64, :],
                                                 AF.Copy, scale=0.0625)
                                    s.activation(qd[64:128, :], ps[q5][64:128, :],
                                                 AF.Copy, scale=0.0625)
                        if oc == 0:
                            @blk.vector
                            def _(v):
                                for q5 in range(4):
                                    v.wait_ge(cx.act.h, ac0 + q5 + 1)
                                    qd = qh[:, q5 * 512:(q5 + 1) * 512]
                                    cqv = cq[:, q5 * 512:(q5 + 1) * 512]
                                    sqv = cq[:, HALF + q5 * 512:HALF + (q5 + 1) * 512]
                                    v.tensor_mul(rr[:, 512:1024], ps[q5][0:64, :], cqv)
                                    v.tensor_mul(rr[:, 1024:1536], rr[0:64, 0:512], sqv)
                                    cx.dve.inc(v.tensor_add(qd[0:64, :], rr[:, 512:1024],
                                                            rr[:, 1024:1536]))
                with nc.Block() as blk:
                    peh0, ach0, axh0, dvh0 = cx.pe.n, cx.act.n, cx.aux.n, cx.dve.n
                    def m_sc(q5, i):
                        return peh0 + 33 * q5 + i + 1
                    def m_bc(q5):
                        return peh0 + 33 * q5 + 33
                    def m_exp(q5, i):
                        return ach0 + 33 * q5 + i + 1
                    def m_rdbc(q5):
                        return ach0 + 33 * q5 + 33
                    def m_den(q5, j):
                        return axh0 + 32 * q5 + j + 1
                    def m_recip(q5):
                        return dvh0 + 3 * q5 + 1
                    def m_mults(q5):
                        return dvh0 + 3 * q5 + 3
                    @blk.tensor
                    def _(te):
                        for q5 in range(4):
                            for i in range(33):
                                if i < 32:
                                    if 33 * q5 + i >= 2:
                                        pi = 33 * q5 + i - 2
                                        pq5, pii = divmod(pi if i >= 2 else 33 * (q5 - 1) + 31 + i, 33)
                                        if i >= 2:
                                            te.wait_ge(cx.act.h, m_exp(q5, i - 2))
                                        else:
                                            te.wait_ge(cx.act.h, m_exp(q5 - 1, 30 + i))
                                    b = ps[i % 2]
                                    te.matmul(b[:],
                                              kfm[:, (2 * kv) * S_FULL + i * 128:
                                                  (2 * kv) * S_FULL + (i + 1) * 128],
                                              qh[:, q5 * 512:(q5 + 1) * 512],
                                              start=True, stop=False)
                                    cx.pe.inc(te.matmul(
                                        b[:],
                                        kfm[:, (2 * kv + 1) * S_FULL + i * 128:
                                            (2 * kv + 1) * S_FULL + (i + 1) * 128],
                                        qh[:, HALF + q5 * 512:HALF + (q5 + 1) * 512],
                                        start=False, stop=True))
                                if i >= 1:
                                    j = i - 1
                                    te.wait_ge(cx.act.h, m_exp(q5, j))
                                    if j == 0 and q5 > 0:
                                        te.wait_ge(cx.dve.h, m_mults(q5 - 1))
                                    e = expt[:, (j % 4) * 512:(j % 4 + 1) * 512]
                                    te.matmul(ps[2][:],
                                              vtm[:, j * 512 + kv * 256:
                                                  j * 512 + kv * 256 + 128],
                                              e, start=(j == 0), stop=(j == 31))
                                    te.matmul(ps[3][:],
                                              vtm[:, j * 512 + kv * 256 + 128:
                                                  j * 512 + kv * 256 + 256],
                                              e, start=(j == 0), stop=(j == 31))
                                    cx.aux.inc(te.matmul(
                                        ps[4][0:1, :], P["ones_col"][:],
                                        e, start=(j == 0), stop=(j == 31)))
                            te.wait_ge(cx.dve.h, m_recip(q5))
                            if q5 > 0:
                                te.wait_ge(cx.act.h, m_rdbc(q5 - 1))
                            cx.pe.inc(te.matmul(ps[5][:], P["ones_row"][:], rden[:],
                                                start=True, stop=True))
                    @blk.scalar
                    def _(s):
                        for q5 in range(4):
                            for i in range(32):
                                s.wait_ge(cx.pe.h, m_sc(q5, i))
                                gi = 32 * q5 + i
                                if gi >= 4:
                                    pq5, pj = divmod(gi - 4, 32)
                                    s.wait_ge(cx.aux.h, m_den(pq5, pj))
                                cx.act.inc(s.activation(
                                    expt[:, (i % 4) * 512:(i % 4 + 1) * 512],
                                    ps[i % 2][:], AF.Exp))
                            s.wait_ge(cx.pe.h, m_bc(q5))
                            cx.act.inc(s.activation(rdbc[:], ps[5][:], AF.Copy))
                    @blk.vector
                    def _(v):
                        for q5 in range(4):
                            v.wait_ge(cx.aux.h, m_den(q5, 31))
                            cx.dve.inc(v.reciprocal(rden[:], ps[4][0:1, 0:512]))
                            v.wait_ge(cx.act.h, m_rdbc(q5))
                            cx.dve.inc(v.tensor_mul(aosl(2 * h, q5), ps[2][:], rdbc[:]))
                            cx.dve.inc(v.tensor_mul(aosl(2 * h + 1, q5), ps[3][:], rdbc[:]))

    x1 = cx.sb([128, NT * DM], F32)
    P["x1"] = x1
    wg_bf_d = nc.dram_tensor("wg_bf_d", [DM, 2048], BF16, kind="Internal").ap()
    wo_bf_d = nc.dram_tensor("wo_bf_d", [2048, DM], BF16, kind="Internal").ap()
    with ExitStack() as ph:
        gw2 = cx.sb([128, 2 * 2048], BF16, ph)
        sg = cx.sb([128, 2 * 512], F32, ph)
        aog = cx.sb([128, 16 * 512], BF16, ph)
        wos = cx.sb([128, 2 * 16 * 128], BF16, ph)
        ysb = cx.sb([128, 2 * 512], F32, ph)

        with nc.Block() as blk:
            @blk.gpsimd
            def _(g):
                for c in range(DC):
                    cx.gd.inc(g.dma_start(
                        wg_bf_d[c * 128:(c + 1) * 128, :],
                        t["wq0"][c * 128:(c + 1) * 128, 2048:4096]), 16)
                P["m_wg0"] = cx.gd.n
                for c in range(16):
                    cx.gd.inc(g.dma_start(
                        wo_bf_d[c * 128:(c + 1) * 128, :],
                        t["wo0"][c * 128:(c + 1) * 128, :]), 16)
                P["m_wo0"] = cx.gd.n

        gw_mk = {}
        def gw_stream(sy, t5_, gbh):
            for c in range(DC):
                cx.dm.inc(sy.dma_start(
                    gw2[:, (gbh % 2) * 2048 + c * 256:
                       (gbh % 2) * 2048 + (c + 1) * 256],
                    wg_bf_d[c * 128:(c + 1) * 128,
                            gbh * 256:(gbh + 1) * 256]), 16)
            gw_mk[(t5_, gbh)] = cx.dm.n

        with nc.Block() as blk:
            @blk.sync
            def _(sy):
                sy.wait_ge(cx.gd.h, P["m_wg0"])
                gw_stream(sy, 0, 0)
                gw_stream(sy, 0, 1)

        def wo_stream(sy, oc):
            for fc in range(16):
                cx.dm.inc(sy.dma_start(
                    wos[:, (oc % 2) * 2048 + fc * 128:
                       (oc % 2) * 2048 + (fc + 1) * 128],
                    wo_bf_d[fc * 128:(fc + 1) * 128,
                            oc * 128:(oc + 1) * 128]), 16)
            wo_mk[oc] = cx.dm.n

        for t5 in range(4):
            wo_mk = {}
            with nc.Block() as blk:
                @blk.sync
                def _(sy):
                    for i in range(4):
                        tt = t5 * 4 + i
                        cx.dm.inc(sy.dma_start(
                            x1[:, tt * DM:(tt + 1) * DM],
                            t["x0"][tt * 128:(tt + 1) * 128, :]), 16)
                    sy.wait_ge(cx.dm.h, cx.dm.n)
            with nc.Block() as blk:
                dm0, pe0, ac0, dv0 = cx.dm.n, cx.pe.n, cx.act.n, cx.dve.n
                @blk.sync
                def _(sy):
                    if t5 == 0:
                        sy.wait_ge(cx.gd.h, P["m_wo0"])
                    wo_stream(sy, 0)
                    wo_stream(sy, 1)
                    for gbh in range(2, 8):
                        sy.wait_ge(cx.pe.h, pe0 + 2 * (gbh - 1))
                        gw_stream(sy, t5, gbh)
                @blk.tensor
                def _(te):
                    for gbh in range(8):
                        te.wait_ge(cx.dm.h, gw_mk[(t5, gbh)])
                        for k in range(2):
                            g_ = gbh * 2 + k
                            if g_ >= 2:
                                te.wait_ge(cx.act.h, ac0 + (g_ - 2) + 1)
                            mm = None
                            for c in range(DC):
                                mm = te.matmul(
                                    ps[g_ % 2][:],
                                    gw2[:, (gbh % 2) * 2048 + c * 256 + k * 128:
                                       (gbh % 2) * 2048 + c * 256 + (k + 1) * 128],
                                    P["xn"][:, c * HALF + t5 * 512:
                                            c * HALF + (t5 + 1) * 512],
                                    start=(c == 0), stop=(c == DC - 1))
                            cx.pe.inc(mm)
                @blk.scalar
                def _(s):
                    for g_ in range(16):
                        s.wait_ge(cx.pe.h, pe0 + g_ + 1)
                        if g_ >= 2:
                            s.wait_ge(cx.dve.h, dv0 + (g_ - 2) + 1)
                        cx.act.inc(s.activation(sg[:, (g_ % 2) * 512:(g_ % 2 + 1) * 512],
                                                ps[g_ % 2][:], AF.Silu))
                @blk.vector
                def _(v):
                    for g_ in range(16):
                        v.wait_ge(cx.act.h, ac0 + g_ + 1)
                        cx.dve.inc(v.tensor_mul(aog[:, g_ * 512:(g_ + 1) * 512],
                                                sg[:, (g_ % 2) * 512:(g_ % 2 + 1) * 512],
                                                aosl(g_, t5)))
            with nc.Block() as blk:
                dm0, pe0, ac0, dv0, ax0 = (cx.dm.n, cx.pe.n, cx.act.n,
                                           cx.dve.n, cx.aux.n)
                @blk.sync
                def _(sy):
                    for oc in range(2, DC):
                        sy.wait_ge(cx.pe.h, pe0 + (oc - 1))
                        wo_stream(sy, oc)
                    if t5 < 3:
                        gw_stream(sy, t5 + 1, 0)
                        gw_stream(sy, t5 + 1, 1)
                def emit_tr(te, oc):
                    te.wait_ge(cx.act.h, ac0 + oc + 1)
                    if oc >= 2:
                        te.wait_ge(cx.dve.h, dv0 + 4 * (oc - 2) + 4)
                    for j in range(4):
                        cx.aux.inc(te.transpose(
                            ps[4 + oc % 2][:, j * 128:(j + 1) * 128],
                            ysb[:, (oc % 2) * 512 + j * 128:
                               (oc % 2) * 512 + (j + 1) * 128],
                            P["idf"][:]))
                @blk.tensor
                def _(te):
                    for oc in range(DC):
                        te.wait_ge(cx.dm.h, wo_mk[oc])
                        if oc >= 2:
                            te.wait_ge(cx.act.h, ac0 + (oc - 2) + 1)
                        mm = None
                        for fc in range(16):
                            mm = te.matmul(
                                ps[oc % 2][:],
                                wos[:, (oc % 2) * 2048 + fc * 128:
                                   (oc % 2) * 2048 + (fc + 1) * 128],
                                aog[:, fc * 512:(fc + 1) * 512],
                                start=(fc == 0), stop=(fc == 15))
                        cx.pe.inc(mm)
                        if oc >= 2:
                            emit_tr(te, oc - 2)
                    emit_tr(te, 6)
                    emit_tr(te, 7)
                @blk.scalar
                def _(s):
                    for oc in range(DC):
                        s.wait_ge(cx.pe.h, pe0 + oc + 1)
                        if oc >= 2:
                            s.wait_ge(cx.aux.h, ax0 + 4 * (oc - 2) + 4)
                        cx.act.inc(s.activation(ysb[:, (oc % 2) * 512:(oc % 2 + 1) * 512],
                                                ps[oc % 2][:], AF.Copy))
                @blk.vector
                def _(v):
                    for oc in range(DC):
                        v.wait_ge(cx.aux.h, ax0 + 4 * oc + 4)
                        for j in range(4):
                            tt = t5 * 4 + j
                            xsl = x1[:, tt * DM + oc * 128:tt * DM + (oc + 1) * 128]
                            cx.dve.inc(v.tensor_add(
                                xsl, xsl, ps[4 + oc % 2][:, j * 128:(j + 1) * 128]))

    if DBG:
        with nc.Block() as blk:
            @blk.gpsimd
            def _(g):
                cx.gd.inc(g.dma_start(P["dbg_ao"], ao[:]), 16)
                cx.gd.inc(g.dma_start(P["dbg_x1a"], x1[:]), 16)
                g.wait_ge(cx.gd.h, cx.gd.n)
    aoes.close()
    emit_l0_ffn(cx, t, P, ps, pst)


def emit_l0_ffn(cx, t, P, ps, pst):
    nc = cx.nc
    x1 = P["x1"]
    ph = ExitStack()
    if True:
        wr = cx.sb([128, DC * 8], BF16, ph)
        rt = cx.sb([128, NT * 8], F32, ph)
        ms1 = cx.sb([128, NT * 8], F32, ph)
        ms2 = cx.sb([128, NT * 8], F32, ph)
        a1 = cx.sb([128, NT * 8], F32, ph)
        a2 = cx.sb([128, NT * 8], F32, ph)
        oh1 = cx.sb([128, NT * 8], F32, ph)
        oh2 = cx.sb([128, NT * 8], F32, ph)
        sc8 = cx.sb([128, NT * 8], F32, ph)

        with nc.Block() as blk:
            @blk.gpsimd
            def _(g):
                for c in range(DC):
                    cx.gd.inc(g.dma_start(wr[:, c * 8:(c + 1) * 8],
                                          t["wr0"][c * 128:(c + 1) * 128, :]), 16)
                g.wait_ge(cx.gd.h, cx.gd.n)
        with nc.Block() as blk:
            @blk.tensor
            def _(te):
                for tt in range(NT):
                    for c in range(DC):
                        te.matmul(ps[0][:, tt * 8:(tt + 1) * 8],
                                  P["xn"][:, c * HALF + tt * 128:c * HALF + (tt + 1) * 128],
                                  wr[:, c * 8:(c + 1) * 8],
                                  start=(c == 0), stop=(c == DC - 1))
        with nc.Block() as blk:
            @blk.vector
            def _(v):
                for tt in range(NT):
                    sl = slice(tt * 8, (tt + 1) * 8)
                    v.tensor_copy(rt[:, sl], ps[0][:, sl])
                    v.memset(ms1[:, sl], -1e30)
                    v.memset(ms2[:, sl], -1e30)
                    m = cx.dve.inc(v.reduce_max(ms1[:, tt * 8:tt * 8 + 1],
                                                rt[:, sl], axis=AX))
                    v.wait_ge(cx.dve.h, m)
                    v.match_replace(out=a1[:, sl], in_to_replace=ms1[:, sl],
                                    in_values=rt[:, sl], imm_value=-1e30)
        with nc.Block() as blk:
            @blk.vector
            def _(v):
                for tt in range(NT):
                    sl = slice(tt * 8, (tt + 1) * 8)
                    m = cx.dve.inc(v.tensor_sub(oh1[:, sl], rt[:, sl], a1[:, sl]))
                    v.wait_ge(cx.dve.h, m)
                    v.tensor_scalar(oh1[:, sl], oh1[:, sl], 1e-29, 1.0,
                                    op0=ALU.mult, op1=ALU.min)
                    m = cx.dve.inc(v.reduce_max(ms2[:, tt * 8:tt * 8 + 1],
                                                a1[:, sl], axis=AX))
                    v.wait_ge(cx.dve.h, m)
                    v.match_replace(out=a2[:, sl], in_to_replace=ms2[:, sl],
                                    in_values=a1[:, sl], imm_value=-1e30)
        with nc.Block() as blk:
            dv0 = cx.dve.n
            @blk.vector
            def _(v):
                for tt in range(NT):
                    sl = slice(tt * 8, (tt + 1) * 8)
                    m = cx.dve.inc(v.tensor_sub(oh2[:, sl], a1[:, sl], a2[:, sl]))
                    v.wait_ge(cx.dve.h, m)
                    v.tensor_scalar(oh2[:, sl], oh2[:, sl], 1e-29, 1.0,
                                    op0=ALU.mult, op1=ALU.min)
                    cx.dve.inc(v.tensor_sub(sc8[:, tt * 8:tt * 8 + 1],
                                            ms2[:, tt * 8:tt * 8 + 1],
                                            ms1[:, tt * 8:tt * 8 + 1]))
            @blk.scalar
            def _(s):
                for tt in range(NT):
                    s.wait_ge(cx.dve.h, dv0 + 2 * (tt + 1))
                    s.activation(sc8[:, tt * 8 + 1:tt * 8 + 2],
                                 sc8[:, tt * 8:tt * 8 + 1], AF.Exp)
        with nc.Block() as blk:
            @blk.vector
            def _(v):
                for tt in range(NT):
                    c0 = tt * 8
                    m = cx.dve.inc(v.tensor_scalar_add(sc8[:, c0 + 2:c0 + 3],
                                                       sc8[:, c0 + 1:c0 + 2], 1.0))
                    v.wait_ge(cx.dve.h, m)
                    m = cx.dve.inc(v.reciprocal(sc8[:, c0 + 3:c0 + 4],
                                                sc8[:, c0 + 2:c0 + 3]))
                    v.wait_ge(cx.dve.h, m)
                    v.tensor_mul(sc8[:, c0 + 4:c0 + 5], sc8[:, c0 + 1:c0 + 2],
                                 sc8[:, c0 + 3:c0 + 4])
                    m = cx.dve.inc(v.tensor_scalar(
                        P["wl0"][:, tt * 8:(tt + 1) * 8],
                        oh1[:, tt * 8:(tt + 1) * 8],
                        sc8[:, c0 + 3:c0 + 4], None, op0=ALU.mult))
                    v.wait_ge(cx.dve.h, m)
                    v.scalar_tensor_tensor(out=P["wl0"][:, tt * 8:(tt + 1) * 8],
                                           in0=oh2[:, tt * 8:(tt + 1) * 8],
                                           scalar=sc8[:, c0 + 4:c0 + 5],
                                           in1=P["wl0"][:, tt * 8:(tt + 1) * 8],
                                           op0=ALU.mult, op1=ALU.add)
        if DBG:
            with nc.Block() as blk:
                @blk.gpsimd
                def _(g):
                    cx.gd.inc(g.dma_start(P["dbg_wl0"], P["wl0"][:]), 16)
                    g.wait_ge(cx.gd.h, cx.gd.n)
        ph.close()
        with ExitStack() as pes:
            wd = cx.sb([128, 16 * DM], BF16, pes)
            wgu = cx.sb([128, 4 * 1024], BF16, pes)
            sgb = cx.sb([128, 2 * 512], F32, pes)
            sgu = cx.sb([128, 16 * HALF], BF16, pes)
            for e in range(E0):
                with nc.Block() as blk:
                    gd0, pe0, ac0, dv0 = cx.gd.n, cx.pe.n, cx.act.n, cx.dve.n
                    @blk.gpsimd
                    def _(g):
                        for ic in range(16):
                            bb = ic % 2
                            if ic >= 2:
                                g.wait_ge(cx.pe.h, pe0 + (ic - 1) * 8)
                            for c in range(DC):
                                cx.gd.inc(g.dma_start(
                                    wgu[:, bb * 2048 + c * 128:bb * 2048 + (c + 1) * 128],
                                    t["wg0"][e, c * 128:(c + 1) * 128,
                                             ic * 128:(ic + 1) * 128]), 16)
                                cx.gd.inc(g.dma_start(
                                    wgu[:, bb * 2048 + 1024 + c * 128:
                                       bb * 2048 + 1024 + (c + 1) * 128],
                                    t["wu0"][e, c * 128:(c + 1) * 128,
                                             ic * 128:(ic + 1) * 128]), 16)
                        for ic in range(16):
                            cx.gd.inc(g.dma_start(
                                wd[:, ic * DM:(ic + 1) * DM],
                                t["wd0"][e, ic * 128:(ic + 1) * 128, :]), 16)
                    @blk.tensor
                    def _(te):
                        for ic in range(16):
                            bb = ic % 2
                            te.wait_ge(cx.gd.h, gd0 + (ic + 1) * DC * 32)
                            for t5 in range(4):
                                g_ = ic * 4 + t5
                                if g_ >= 2:
                                    te.wait_ge(cx.dve.h, dv0 + (g_ - 2) + 1)
                                mm = None
                                for c in range(DC):
                                    mm = te.matmul(
                                        ps[g_ % 2][:],
                                        wgu[:, bb * 2048 + c * 128:bb * 2048 + (c + 1) * 128],
                                        P["xn"][:, c * HALF + t5 * 512:c * HALF + (t5 + 1) * 512],
                                        start=(c == 0), stop=(c == DC - 1))
                                cx.pe.inc(mm)
                                mm = None
                                for c in range(DC):
                                    mm = te.matmul(
                                        ps[2 + g_ % 2][:],
                                        wgu[:, bb * 2048 + 1024 + c * 128:
                                           bb * 2048 + 1024 + (c + 1) * 128],
                                        P["xn"][:, c * HALF + t5 * 512:c * HALF + (t5 + 1) * 512],
                                        start=(c == 0), stop=(c == DC - 1))
                                cx.pe.inc(mm)
                    @blk.scalar
                    def _(s):
                        for g_ in range(64):
                            s.wait_ge(cx.pe.h, pe0 + 2 * g_ + 1)
                            if g_ >= 2:
                                s.wait_ge(cx.dve.h, dv0 + (g_ - 2) + 1)
                            cx.act.inc(s.activation(sgb[:, (g_ % 2) * 512:(g_ % 2 + 1) * 512],
                                                    ps[g_ % 2][:], AF.Silu))
                    @blk.vector
                    def _(v):
                        for g_ in range(64):
                            ic, t5 = divmod(g_, 4)
                            v.wait_ge(cx.act.h, ac0 + g_ + 1)
                            v.wait_ge(cx.pe.h, pe0 + 2 * g_ + 2)
                            cx.dve.inc(v.tensor_mul(
                                sgu[:, ic * HALF + t5 * 512:ic * HALF + (t5 + 1) * 512],
                                sgb[:, (g_ % 2) * 512:(g_ % 2 + 1) * 512],
                                ps[2 + g_ % 2][:]))
                gdw = cx.gd.n
                with nc.Block() as blk:
                    pe0, dv0 = cx.pe.n, cx.dve.n
                    @blk.tensor
                    def _(te):
                        te.wait_ge(cx.gd.h, gdw)
                        gidx = 0
                        for tt in range(NT):
                            for hf in range(2):
                                if gidx >= 2:
                                    te.wait_ge(cx.dve.h, dv0 + (gidx - 2) + 1)
                                mm = None
                                for ic in range(16):
                                    mm = te.matmul(
                                        ps[4 + gidx % 2][:],
                                        sgu[:, ic * HALF + tt * 128:ic * HALF + (tt + 1) * 128],
                                        wd[:, ic * DM + hf * 512:ic * DM + (hf + 1) * 512],
                                        start=(ic == 0), stop=(ic == 15))
                                cx.pe.inc(mm)
                                gidx += 1
                    @blk.vector
                    def _(v):
                        gidx = 0
                        for tt in range(NT):
                            for hf in range(2):
                                v.wait_ge(cx.pe.h, pe0 + gidx + 1)
                                xsl = x1[:, tt * DM + hf * 512:tt * DM + (hf + 1) * 512]
                                cx.dve.inc(v.scalar_tensor_tensor(
                                    out=xsl, in0=ps[4 + gidx % 2][:],
                                    scalar=P["wl0"][:, tt * 8 + e:tt * 8 + e + 1],
                                    in1=xsl, op0=ALU.mult, op1=ALU.add))
                                gidx += 1



def host_constants(core):
    par = core % 2
    pos = np.concatenate([np.arange(par * HALF, (par + 1) * HALF),
                          np.arange((1 - par) * HALF, (2 - par) * HALF)]).astype(np.float32)
    inv = np.power(np.float32(10000.0),
                   -(np.arange(0, RD, 2, dtype=np.float32) / np.float32(RD)))
    freqs = pos[:, None] * inv[None, :]
    emb = np.concatenate([freqs, freqs], axis=1)
    cosk = np.ascontiguousarray(np.cos(emb).T)
    sink = np.ascontiguousarray(np.sin(emb).T)
    return {
        "cosk": cosk, "sink": sink,
        "cosq": np.ascontiguousarray(cosk[:, :HALF] * np.float32(0.0625)),
        "sinq": np.ascontiguousarray(sink[:, :HALF] * np.float32(0.0625)),
    }


def declare_l0_inputs(nc):
    t = {}
    t["x0"] = nc.dram_tensor("x0", [S_FULL, DM], F32, kind="ExternalInput").ap()
    t["wq0"] = nc.dram_tensor("wq0", [DM, 4096], F32, kind="ExternalInput").ap()
    t["wk0"] = nc.dram_tensor("wk0", [DM, 512], F32, kind="ExternalInput").ap()
    t["wv0"] = nc.dram_tensor("wv0", [DM, 512], F32, kind="ExternalInput").ap()
    t["wo0"] = nc.dram_tensor("wo0", [2048, DM], F32, kind="ExternalInput").ap()
    t["wr0"] = nc.dram_tensor("wr0", [DM, E0], F32, kind="ExternalInput").ap()
    t["wg0"] = nc.dram_tensor("wg0", [E0, DM, I0], F32, kind="ExternalInput").ap()
    t["wu0"] = nc.dram_tensor("wu0", [E0, DM, I0], F32, kind="ExternalInput").ap()
    t["wd0"] = nc.dram_tensor("wd0", [E0, I0, DM], F32, kind="ExternalInput").ap()
    for n in ("cosk", "sink"):
        t[n] = nc.dram_tensor(n, [RD, S_FULL], F32, kind="ExternalInput").ap()
    for n in ("cosq", "sinq"):
        t[n] = nc.dram_tensor(n, [RD, HALF], F32, kind="ExternalInput").ap()
    return t


def build_l0_program():
    nc = bass.Bass("TRN2", target_bir_lowering=False, debug=False)
    es = ExitStack()
    cx = Cx(nc, es)
    t = declare_l0_inputs(nc)
    y = nc.dram_tensor("x1out", [HALF, DM], F32, kind="ExternalOutput").ap()
    P = {}
    if DBG:
        P["dbg_xn"] = nc.dram_tensor("dbg_xn", [128, DC * HALF], F32, kind="ExternalOutput").ap()
        P["dbg_kfm"] = nc.dram_tensor("dbg_kfm", [128, 4 * S_FULL], F32, kind="ExternalOutput").ap()
        P["dbg_vtm"] = nc.dram_tensor("dbg_vtm", [128, 32 * 512], F32, kind="ExternalOutput").ap()
        P["dbg_ao"] = nc.dram_tensor("dbg_ao", [128, 16 * HALF], F32, kind="ExternalOutput").ap()
        P["dbg_x1a"] = nc.dram_tensor("dbg_x1a", [128, NT * DM], F32, kind="ExternalOutput").ap()
        P["dbg_wl0"] = nc.dram_tensor("dbg_wl0", [128, NT * E0], F32, kind="ExternalOutput").ap()
    P["idf"] = cx.sb([128, 128], F32)
    P["idb"] = cx.sb([128, 128], BF16)
    P["ones_row"] = cx.sb([1, 128], F32)
    P["ones_col"] = cx.sb([128, 1], BF16)
    P["xn"] = cx.sb([128, DC * HALF], BF16)
    P["wl0"] = cx.sb([128, NT * E0], F32)
    ps = [es.enter_context(nc.psum_tensor(f"ps{i}", [128, 512], F32)) for i in range(6)]
    pst = [es.enter_context(nc.psum_tensor("pstA", [128, 1024], BF16)),
           es.enter_context(nc.psum_tensor("pstB", [128, 1024], BF16))]
    emit_identity_ones(nc, P, cx)
    emit_l0(cx, t, P, ps, pst)
    with nc.Block() as blk:
        @blk.sync
        def _(sy):
            for tt in range(NT):
                cx.dm.inc(sy.dma_start(y[tt * 128:(tt + 1) * 128, :],
                                       P["x1"][:, tt * DM:(tt + 1) * DM]), 16)
            sy.wait_ge(cx.dm.h, cx.dm.n)
    es.close()
    return nc


def l0_in_map(core, inputs):
    b, par = core // 2, core % 2
    x = np.asarray(inputs["L0_attn_hidden_states"][b])
    xr = np.concatenate([x[par * HALF:(par + 1) * HALF],
                         x[(1 - par) * HALF:(2 - par) * HALF]], axis=0)
    d = {"x0": np.ascontiguousarray(xr),
         "wq0": inputs["L0_attn_Wq"], "wk0": inputs["L0_attn_Wk"],
         "wv0": inputs["L0_attn_Wv"], "wo0": inputs["L0_attn_Wo"],
         "wr0": inputs["L0_ffn_Wr"], "wg0": inputs["L0_ffn_Wg"],
         "wu0": inputs["L0_ffn_Wu"], "wd0": inputs["L0_ffn_Wd"]}
    d.update(host_constants(core))
    return {k: np.ascontiguousarray(np.asarray(v, dtype=np.float32)) for k, v in d.items()}


def kernel(**inputs):
    return kernel_fused(**inputs)


def kernel_unfused(**inputs):
    nc0 = build_l0_program()
    in_maps = [l0_in_map(c, inputs) for c in range(NC)]
    res = run_bass_kernel_spmd(nc0, in_maps, core_ids=list(range(NC)))
    x1 = np.empty((NB, S_FULL, DM), np.float32)
    for c in range(NC):
        b, par = c // 2, c % 2
        x1[b, par * HALF:(par + 1) * HALF] = res.results[c]["x1out"]
    nc1 = build_l1_program()
    in_maps = [l1_in_map(c, inputs, x1) for c in range(NC)]
    res = run_bass_kernel_spmd(nc1, in_maps, core_ids=list(range(NC)))
    out = np.empty((NB, S_FULL, DM), np.float32)
    for c in range(NC):
        b, par = c // 2, c % 2
        out[b, par * HALF:(par + 1) * HALF] = res.results[c]["yout"]
    return out



def emit_l1(cx, t, P, ps, pst, fused=False):
    nc = cx.nc
    xn1 = P["xn"]
    va_d = nc.dram_tensor("va_d", [NT * 128, 1040], BF16, kind="Internal").ap()
    wqbf_d = nc.dram_tensor("wqbf_d", [DM, DM], BF16, kind="Internal").ap()
    wkbf_d = nc.dram_tensor("wkbf_d", [DM, DM], BF16, kind="Internal").ap()
    wvbf_d = nc.dram_tensor("wvbf_d", [DM, DM], BF16, kind="Internal").ap()
    wobf_d = nc.dram_tensor("wobf_d", [DM, DM], BF16, kind="Internal").ap()
    ewbf_d = nc.dram_tensor("ewbf_d", [E1 * DM, DM], BF16, kind="Internal").ap()
    P["ewbf_d"] = ewbf_d

    bqb = cx.sb([128, DM], F32)
    bo_fm = cx.sb([128, 8], F32)
    gbb = cx.sb([128, 8], F32)
    gwT = cx.sb([128, DC * 8], BF16)
    masku = cx.sb([128, 128], BF16)
    S_m = cx.sb([128, 8 * 65], F32)
    S_bf = cx.sb([128, 8 * 65], BF16)
    Srep = cx.sb([128, 8 * 65], F32)
    wl1 = cx.sb([128, NT * 8], F32)
    P["gwT"], P["gbb"], P["wl1"] = gwT, gbb, wl1
    ktm = cx.sb([128, NT * 1024], BF16)
    is2 = cx.sb([128, 1], F32)

    c23 = ExitStack()
    bkb = cx.sb([128, DM], F32, c23)
    bvb = cx.sb([128, DM], F32, c23)
    with ExitStack() as ph:
        brow = cx.sb([1, 3 * DM + 8], F32, ph)
        bo8 = cx.sb([8, 128], F32, ph)
        gw8 = cx.sb([8, DM], BF16, ph)
        wtmp = cx.sb([128, 2 * DM], BF16, ph)
        wout = cx.sb([128, DC * 128], BF16, ph)

        with nc.Block() as blk:
            @blk.sync
            def _(sy):
                cx.dm.inc(sy.dma_start(brow[0:1, 0:DM], t["bq1"].rearrange("(o a) -> o a", o=1)), 16)
                cx.dm.inc(sy.dma_start(brow[0:1, DM:2 * DM], t["bk1"].rearrange("(o a) -> o a", o=1)), 16)
                cx.dm.inc(sy.dma_start(brow[0:1, 2 * DM:3 * DM], t["bv1"].rearrange("(o a) -> o a", o=1)), 16)
                cx.dm.inc(sy.dma_start(brow[0:1, 3 * DM:3 * DM + 8], t["gb1"].rearrange("(o a) -> o a", o=1)), 16)
                cx.dm.inc(sy.dma_start(bo8[:], t["bo1"].rearrange("(a b) -> a b", b=128)), 16)
                cx.dm.inc(sy.dma_start(is2[:], t["is2nd"]), 16)
                sy.wait_ge(cx.dm.h, cx.dm.n)
            @blk.gpsimd
            def _(g):
                cx.gd.inc(g.dma_start(gw8[:], t["gw1"]), 16)
                cx.gd.inc(g.dma_start(masku[:], t["masku"]), 16)
                g.wait_ge(cx.gd.h, cx.gd.n)
            @blk.vector
            def _(v):
                v.memset(Srep[:], 0.0)
        with nc.Block() as blk:
            pe0 = cx.pe.n
            @blk.tensor
            def _(te):
                for i in range(3):
                    for hf in range(2):
                        cx.pe.inc(te.matmul(
                            ps[2 * i + hf][:], P["ones_row"][:],
                            brow[0:1, i * DM + hf * 512:i * DM + (hf + 1) * 512],
                            start=True, stop=True))
            @blk.scalar
            def _(s):
                dsts = [bqb, bkb, bvb]
                for i in range(3):
                    for hf in range(2):
                        s.wait_ge(cx.pe.h, pe0 + 2 * i + hf + 1)
                        s.activation(dsts[i][:, hf * 512:(hf + 1) * 512],
                                     ps[2 * i + hf][:], AF.Copy)
        with nc.Block() as blk:
            pe0 = cx.pe.n
            @blk.tensor
            def _(te):
                cx.pe.inc(te.matmul(ps[0][:, 0:8], P["ones_row"][:],
                                    brow[0:1, 3 * DM:3 * DM + 8],
                                    start=True, stop=True))
            @blk.scalar
            def _(s):
                s.wait_ge(cx.pe.h, pe0 + 1)
                s.activation(gbb[:], ps[0][:, 0:8], AF.Copy)
        with nc.Block() as blk:
            pe0, dv0 = cx.pe.n, cx.dve.n
            @blk.tensor
            def _(te):
                cx.pe.inc(te.transpose(ps[5][:, 0:8], bo8[:], P["idf"][0:8, 0:8]))
                for c in range(DC):
                    if c >= 2:
                        te.wait_ge(cx.dve.h, dv0 + (c - 2) + 1)
                    cx.pe.inc(te.transpose(pst[c % 2][:, 0:8],
                                           gw8[:, c * 128:(c + 1) * 128],
                                           P["idb"][0:8, 0:8]))
            @blk.vector
            def _(v):
                v.wait_ge(cx.pe.h, pe0 + 1)
                v.tensor_copy(bo_fm[:], ps[5][:, 0:8])
                for c in range(DC):
                    v.wait_ge(cx.pe.h, pe0 + c + 2)
                    cx.dve.inc(v.tensor_copy(gwT[:, c * 8:(c + 1) * 8],
                                             pst[c % 2][:, 0:8]))
        with nc.Block() as blk:
            @blk.gpsimd
            def _(g):
                P["m_wk"] = cx.gd.inc(g.dma_start(wkbf_d, t["wk1"]), 16)
                P["m_wv"] = cx.gd.inc(g.dma_start(wvbf_d, t["wv1"]), 16)
                P["m_wq"] = cx.gd.inc(g.dma_start(wqbf_d, t["wq1"]), 16)
                P["m_wo"] = cx.gd.inc(g.dma_start(wobf_d, t["wo1"]), 16)
                for e in range(E1):
                    cx.gd.inc(g.dma_start(ewbf_d[e * DM:(e + 1) * DM, :],
                                          t["ew1"][e]), 16)
                P["m_ew"] = cx.gd.n

    if fused:
        emit_l1_kv_fused(cx, t, P, ps, pst, wkbf_d, wvbf_d, bkb, bvb,
                         ktm, Srep, va_d)
    if not fused:
     with ExitStack() as ph:
        xtmp = cx.sb([128, DM], F32, ph)
        xnt = cx.sb([128, DM], BF16, ph)
        sq = cx.sb([128, DM], F32, ph)
        stw = cx.sb([128, 8], F32, ph)
        xfm = cx.sb([128, DM], BF16, ph)
        wstr = cx.sb([128, 2 * DM], BF16, ph)
        qf = cx.sb([128, DM], F32, ph)
        rb = cx.sb([128, 512], F32, ph)
        mn = cx.sb([128, 512], F32, ph)
        eb = cx.sb([128, 512], F32, ph)
        kb = cx.sb([128, DM], BF16, ph)
        vb = cx.sb([128, DM], BF16, ph)
        vao = cx.sb([128, 1040], BF16, ph)

        with nc.Block() as blk:
            @blk.vector
            def _(v):
                v.memset(vao[:], 1.0)

        for half in ([0] if fused else [0, 1]):
            for ti in range(NT):
                row0 = half * HALF + ti * 128
                if fused:
                    xsrc = P["x2"][:, ti * DM:(ti + 1) * DM]
                else:
                    with nc.Block() as blk:
                        @blk.sync
                        def _(sy):
                            cx.dm.inc(sy.dma_start(xtmp[:],
                                                   t["x1r"][row0:row0 + 128, :]), 16)
                            sy.wait_ge(cx.dm.h, cx.dm.n)
                    xsrc = xtmp[:]
                emit_ln_blocks(cx, lambda i: xsrc, lambda i: xnt[:], sq, stw, 1)
                if half == 0:
                    jobs = [(xnt[:, c * 128:(c + 1) * 128],
                             xn1[:, c * HALF + ti * 128:c * HALF + (ti + 1) * 128])
                            for c in range(DC)]
                else:
                    jobs = [(xnt[:, c * 128:(c + 1) * 128],
                             xfm[:, c * 128:(c + 1) * 128]) for c in range(DC)]
                emit_transpose_block(cx, P, pst, jobs)
                if half == 0:
                    def lhs(c, ti=ti):
                        return xn1[:, c * HALF + ti * 128:c * HALF + (ti + 1) * 128]
                else:
                    def lhs(c):
                        return xfm[:, c * 128:(c + 1) * 128]
                for wd_d, bias_bc, dst, phi in ((wkbf_d, bkb, kb, True),
                                                (wvbf_d, bvb, vb, False)):
                    with nc.Block() as blk:
                        pe0, dv0, ac0 = cx.pe.n, cx.dve.n, cx.act.n
                        dm0 = cx.dm.n
                        @blk.sync
                        def _(sy):
                            for hc in range(DC):
                                if hc >= 2:
                                    sy.wait_ge(cx.pe.h, pe0 + (hc - 1))
                                cx.dm.inc(sy.dma_start_transpose(
                                    wstr[:, (hc % 2) * DM:(hc % 2 + 1) * DM],
                                    wd_d[:, hc * 128:(hc + 1) * 128]), 16)
                        @blk.tensor
                        def _(te):
                            for hc in range(DC):
                                te.wait_ge(cx.dm.h, dm0 + (hc + 1) * 16)
                                mm = None
                                for hf in range(2):
                                    mm = te.matmul(
                                        ps[hf][:], lhs(hc),
                                        wstr[:, (hc % 2) * DM + hf * 512:
                                             (hc % 2) * DM + (hf + 1) * 512],
                                        start=(hc == 0), stop=(hc == DC - 1))
                                cx.pe.inc(mm)
                        @blk.vector
                        def _(v):
                            v.wait_ge(cx.pe.h, pe0 + DC)
                            cx.dve.inc(v.tensor_add(qf[:, 0:512], ps[0][:],
                                                    bias_bc[:, 0:512]))
                            cx.dve.inc(v.tensor_add(qf[:, 512:1024], ps[1][:],
                                                    bias_bc[:, 512:1024]))
                            if not phi:
                                v.tensor_copy(dst[:], qf[:])
                            else:
                                for hf in range(2):
                                    sl = slice(hf * 512, (hf + 1) * 512)
                                    v.tensor_relu(rb[:], qf[:, sl])
                                    cx.dve.inc(v.tensor_scalar_min(mn[:], qf[:, sl], 0.0))
                                    v.wait_ge(cx.act.h, ac0 + hf + 1)
                                    cx.dve.inc(v.tensor_add(dst[:, sl], eb[:], rb[:]))
                        if phi:
                            @blk.scalar
                            def _(s):
                                for hf in range(2):
                                    s.wait_ge(cx.dve.h, dv0 + 3 + 2 * hf)
                                    cx.act.inc(s.activation(eb[:], mn[:], AF.Exp))
                do_state = (half == 1) or fused
                with nc.Block() as blk:
                    pe0, dv0 = cx.pe.n, cx.dve.n

                    @blk.vector
                    def _(v):
                        if half == 0:
                            v.tensor_copy(ktm[:, ti * 1024:(ti + 1) * 1024], kb[:])
                        for h in range(H1):
                            cx.dve.inc(v.tensor_copy(
                                vao[:, h * 65:h * 65 + 64],
                                vb[:, h * 64:(h + 1) * 64]))
                        if do_state:
                            for h in range(H1):
                                v.wait_ge(cx.pe.h, pe0 + h + 1)
                                base, co = (h % 2) * 64, (h // 2) * 65
                                dstv = Srep[base:base + 64, co:co + 65]
                                cx.dve.inc(v.tensor_add(dstv, dstv,
                                                        ps[4 + h % 2][0:64, 0:65]))
                    if do_state:
                        @blk.tensor
                        def _(te):
                            for h in range(H1):
                                te.wait_ge(cx.dve.h, dv0 + h + 1)
                                if h >= 2:
                                    te.wait_ge(cx.dve.h, dv0 + 16 + (h - 2) + 1)
                                cx.pe.inc(te.matmul(
                                    ps[4 + h % 2][0:64, 0:65],
                                    kb[:, h * 64:(h + 1) * 64],
                                    vao[:, h * 65:(h + 1) * 65],
                                    start=True, stop=True))
                if half == 0:
                    with nc.Block() as blk:
                        @blk.sync
                        def _(sy):
                            cx.dm.inc(sy.dma_start(
                                va_d[ti * 128:(ti + 1) * 128, :], vao[:]), 16)
                            sy.wait_ge(cx.dm.h, cx.dm.n)
    c23.close()

    if "x2" not in P:
        P["x2"] = cx.sb([128, NT * DM], F32)
        with nc.Block() as blk:
            @blk.sync
            def _(sy):
                for tt in range(NT):
                    cx.dm.inc(sy.dma_start(P["x2"][:, tt * DM:(tt + 1) * DM],
                                           t["x1r"][tt * 128:(tt + 1) * 128, :]), 16)
                sy.wait_ge(cx.dm.h, cx.dm.n)
    x2 = P["x2"]

    hses = ExitStack()
    if fused:
        emit_handshake_send(cx, t, P, Srep, hses)

    emit_l1_ffn(cx, t, P, ps, pst)

    if fused:
        emit_handshake_recv(cx, t, P, Srep)
    hses.close()

    with nc.Block() as blk:
        dv0 = cx.dve.n
        @blk.vector
        def _(v):
            cx.dve.inc(v.tensor_scalar(S_m[:], Srep[:], is2[:, 0:1], None,
                                       op0=ALU.mult))
            v.wait_ge(cx.dve.h, dv0 + 1)
            v.tensor_copy(S_bf[:], S_m[:])

    with ExitStack() as ph:
        vstr = cx.sb([128, 2 * 1040], BF16, ph)
        qf = cx.sb([128, DM], F32, ph)
        rb = cx.sb([128, 512], F32, ph)
        mn = cx.sb([128, 512], F32, ph)
        eb = cx.sb([128, 512], F32, ph)
        qb = cx.sb([128, DM], BF16, ph)
        qfm = cx.sb([128, DM], BF16, ph)
        kfj = cx.sb([128, DM], BF16, ph)
        scm = cx.sb([128, 2 * 256], BF16, ph)
        rden = cx.sb([128, 4], F32, ph)
        oub = cx.sb([128, DM], BF16, ph)
        ofm = cx.sb([128, DM], BF16, ph)
        ysb = cx.sb([128, 8 * 128], F32, ph)
        wor = cx.sb([128, DC * DM], BF16, ph)
        wqr = cx.sb([128, DC * DM], BF16, ph)

        v_mk = {}
        def v_load(sy, j_):
            cx.dm.inc(sy.dma_start(
                vstr[:, (j_ % 2) * 1040:(j_ % 2 + 1) * 1040],
                va_d[j_ * 128:(j_ + 1) * 128, :]), 16)
            v_mk[j_] = cx.dm.n

        with nc.Block() as blk:
            @blk.sync
            def _(sy):
                sy.wait_ge(cx.gd.h, P["m_wq"])
                for hc in range(DC):
                    cx.dm.inc(sy.dma_start_transpose(
                        wqr[:, hc * DM:(hc + 1) * DM],
                        wqbf_d[:, hc * 128:(hc + 1) * 128]), 16)
                m_wqrl_ = cx.dm.n
                v_load(sy, 0)
                sy.wait_ge(cx.gd.h, P["m_wo"])
                for dmc in range(DC):
                    cx.dm.inc(sy.dma_start_transpose(
                        wor[:, dmc * DM:(dmc + 1) * DM],
                        wobf_d[:, dmc * 128:(dmc + 1) * 128]), 16)
        m_wqrl = v_mk[0] - 16
        m_worl = cx.dm.n

        for j in range(NT):
            tcol = slice(j * 128, (j + 1) * 128)
            with nc.Block() as blk:
                pe0, dv0, ac0 = cx.pe.n, cx.dve.n, cx.act.n
                dm0 = cx.dm.n
                @blk.sync
                def _(sy):
                    if j + 1 < NT:
                        v_load(sy, j + 1)
                @blk.tensor
                def _(te):
                    if j == 0:
                        te.wait_ge(cx.dm.h, m_wqrl)
                    te.wait_ge(cx.dm.h, v_mk[j])
                    for hc in range(DC):
                        mm = None
                        for hf in range(2):
                            mm = te.matmul(
                                ps[hf][:],
                                xn1[:, hc * HALF + j * 128:hc * HALF + (j + 1) * 128],
                                wqr[:, hc * DM + hf * 512:
                                    hc * DM + (hf + 1) * 512],
                                start=(hc == 0), stop=(hc == DC - 1))
                        cx.pe.inc(mm)
                @blk.vector
                def _(v):
                    v.wait_ge(cx.pe.h, pe0 + DC)
                    cx.dve.inc(v.tensor_add(qf[:, 0:512], ps[0][:], bqb[:, 0:512]))
                    cx.dve.inc(v.tensor_add(qf[:, 512:1024], ps[1][:], bqb[:, 512:1024]))
                    for hf in range(2):
                        sl = slice(hf * 512, (hf + 1) * 512)
                        v.tensor_relu(rb[:], qf[:, sl])
                        cx.dve.inc(v.tensor_scalar_min(mn[:], qf[:, sl], 0.0))
                        v.wait_ge(cx.act.h, ac0 + hf + 1)
                        cx.dve.inc(v.tensor_add(qb[:, sl], eb[:], rb[:]))
                @blk.scalar
                def _(s):
                    for hf in range(2):
                        s.wait_ge(cx.dve.h, dv0 + 3 + 2 * hf)
                        cx.act.inc(s.activation(eb[:], mn[:], AF.Exp))
            jobs = []
            for p in range(8):
                jobs.append((qb[:, p * 128:(p + 1) * 128], qfm[:, p * 128:(p + 1) * 128]))
                jobs.append((ktm[:, j * 1024 + p * 128:j * 1024 + (p + 1) * 128],
                             kfj[:, p * 128:(p + 1) * 128]))
            emit_transpose_block(cx, P, pst, jobs)
            with nc.Block() as blk:
                pe0, dv0, ac0 = cx.pe.n, cx.dve.n, cx.act.n
                NP = H1 // 2
                mk_sc, mk_num, mk_st = {}, {}, {}
                @blk.tensor
                def _(te):
                    def sc(p):
                        if p >= 1:
                            te.wait_ge(cx.dve.h, dv0 + 6 * (p - 1) + 1)
                        mm = None
                        for hb in range(2):
                            mm = te.matmul(
                                ps[hb][:, 0:128],
                                kfj[hb * 64:(hb + 1) * 64, p * 128:(p + 1) * 128],
                                qfm[hb * 64:(hb + 1) * 64, p * 128:(p + 1) * 128],
                                start=True, stop=True)
                        cx.pe.inc(mm)
                        mk_sc[p] = cx.pe.n
                    def num(p):
                        g, co = p % 2, p * 65
                        te.wait_ge(cx.dve.h, dv0 + 6 * p + 1)
                        if p >= 2:
                            te.wait_ge(cx.act.h, ac0 + 2 * p - 2)
                        mm = None
                        for hb in range(2):
                            h = 2 * p + hb
                            te.matmul(
                                ps[2 + g][:, hb * 128:hb * 128 + 65],
                                scm[:, g * 256 + hb * 128:g * 256 + (hb + 1) * 128],
                                vstr[:, (j % 2) * 1040 + h * 65:(j % 2) * 1040 + (h + 1) * 65],
                                start=True, stop=False)
                            mm = te.matmul(
                                ps[2 + g][:, hb * 128:hb * 128 + 65],
                                qfm[hb * 64:(hb + 1) * 64, p * 128:(p + 1) * 128],
                                S_bf[hb * 64:(hb + 1) * 64, co:co + 65],
                                start=False, stop=True)
                        cx.pe.inc(mm)
                        mk_num[p] = cx.pe.n
                    def st(p):
                        g = p % 2
                        if p >= 2:
                            te.wait_ge(cx.dve.h, dv0 + 6 * (p - 2) + 5)
                        mm = None
                        for hb in range(2):
                            h = 2 * p + hb
                            mm = te.matmul(
                                ps[4 + g][0:64, hb * 128:hb * 128 + 65],
                                ktm[:, j * 1024 + h * 64:j * 1024 + (h + 1) * 64],
                                vstr[:, (j % 2) * 1040 + h * 65:(j % 2) * 1040 + (h + 1) * 65],
                                start=True, stop=True)
                        cx.pe.inc(mm)
                        mk_st[p] = cx.pe.n
                    for p in range(NP):
                        sc(p)
                        num(p)
                        st(p)
                @blk.vector
                def _(v):
                    for p in range(NP):
                        g, co = p % 2, p * 65
                        v.wait_ge(cx.pe.h, mk_sc[p])
                        v.tensor_mul(scm[:, g * 256:g * 256 + 128],
                                     ps[0][:, 0:128], masku[:])
                        cx.dve.inc(v.tensor_mul(scm[:, g * 256 + 128:g * 256 + 256],
                                                ps[1][:, 0:128], masku[:]))
                        v.wait_ge(cx.pe.h, mk_num[p])
                        if p >= 2:
                            v.wait_ge(cx.act.h, ac0 + 2 * p - 2)
                        cx.dve.inc(v.reciprocal(rden[:, 2 * g:2 * g + 1],
                                                ps[2 + g][:, 64:65]))
                        v.wait_ge(cx.dve.h, cx.dve.n)
                        cx.dve.inc(v.reciprocal(rden[:, 2 * g + 1:2 * g + 2],
                                                ps[2 + g][:, 192:193]))
                        v.wait_ge(cx.pe.h, mk_st[p])
                        d0 = S_m[0:64, co:co + 65]
                        cx.dve.inc(v.tensor_add(d0, d0, ps[4 + g][0:64, 0:65]))
                        d1 = S_m[64:128, co:co + 65]
                        cx.dve.inc(v.tensor_add(d1, d1, ps[4 + g][0:64, 128:193]))
                        m = cx.dve.n
                        v.wait_ge(cx.dve.h, m)
                        cx.dve.inc(v.tensor_copy(S_bf[:, co:co + 65],
                                                 S_m[:, co:co + 65]))
                @blk.scalar
                def _(s):
                    for p in range(NP):
                        g = p % 2
                        s.wait_ge(cx.dve.h, dv0 + 6 * p + 3)
                        cx.act.inc(s.activation(
                            oub[:, (2 * p) * 64:(2 * p) * 64 + 64],
                            ps[2 + g][:, 0:64], AF.Copy,
                            scale=rden[:, 2 * g:2 * g + 1]))
                        cx.act.inc(s.activation(
                            oub[:, (2 * p + 1) * 64:(2 * p + 1) * 64 + 64],
                            ps[2 + g][:, 128:192], AF.Copy,
                            scale=rden[:, 2 * g + 1:2 * g + 2]))
            with nc.Block() as blk:
                pe0, dv0, ac0, axp = cx.pe.n, cx.dve.n, cx.act.n, cx.aux.n
                @blk.tensor
                def _(te):
                    for gtr in range(2):
                        for q in range(4):
                            p = gtr * 4 + q
                            cx.aux.inc(te.transpose(
                                pst[gtr % 2][:, q * 128:(q + 1) * 128],
                                oub[:, p * 128:(p + 1) * 128], P["idb"][:]))
                    if j == 0:
                        te.wait_ge(cx.dm.h, m_worl)
                    for dmc in range(DC):
                        te.wait_ge(cx.dve.h, dv0 + 4 * (dmc // 4) + 4)
                        mm = None
                        for oc in range(DC):
                            mm = te.matmul(
                                ps[oc // 4][:, (oc % 4) * 128:(oc % 4 + 1) * 128],
                                wor[:, dmc * DM + oc * 128:
                                    dmc * DM + (oc + 1) * 128],
                                ofm[:, dmc * 128:(dmc + 1) * 128],
                                start=(dmc == 0 and oc % 4 == 0), stop=(dmc == DC - 1))
                        cx.pe.inc(mm)
                    for g in range(2):
                        te.wait_ge(cx.dve.h, dv0 + 8 + 4 * (g + 1))
                        for q in range(4):
                            oc = g * 4 + q
                            cx.aux.inc(te.transpose(
                                ps[2 + g % 2][:, q * 128:(q + 1) * 128],
                                ysb[:, oc * 128:(oc + 1) * 128], P["idf"][:]))
                @blk.vector
                def _(v):
                    for gtr in range(2):
                        v.wait_ge(cx.aux.h, axp + 4 * (gtr + 1))
                        for q in range(4):
                            p = gtr * 4 + q
                            cx.dve.inc(v.tensor_copy(
                                ofm[:, p * 128:(p + 1) * 128],
                                pst[gtr % 2][:, q * 128:(q + 1) * 128]))
                    for oc in range(DC):
                        v.wait_ge(cx.pe.h, pe0 + DC)
                        cx.dve.inc(v.tensor_scalar(
                            ysb[:, oc * 128:(oc + 1) * 128],
                            ps[oc // 4][:, (oc % 4) * 128:(oc % 4 + 1) * 128],
                            bo_fm[:, oc:oc + 1], None, op0=ALU.add))
                    ax0 = cx.aux.n - 8
                    for g in range(2):
                        v.wait_ge(cx.aux.h, ax0 + 4 * (g + 1))
                        for q in range(4):
                            oc = g * 4 + q
                            xsl = x2[:, j * DM + oc * 128:j * DM + (oc + 1) * 128]
                            cx.dve.inc(v.tensor_add(xsl, xsl,
                                                    ps[2 + g % 2][:, q * 128:(q + 1) * 128]))


def emit_l1_kv_fused(cx, t, P, ps, pst, wkbf_d, wvbf_d, bkb, bvb,
                     ktm, Srep, va_d):
    nc = cx.nc
    xn1 = P["xn"]
    x2 = P["x2"]
    with ExitStack() as ph:
        xnt = cx.sb([128, 4 * DM], BF16, ph)
        sq = cx.sb([128, DM], F32, ph)
        stw = cx.sb([128, 4 * 8], F32, ph)
        wres = cx.sb([128, DC * DM], BF16, ph)
        rb = cx.sb([128, 512], F32, ph)
        mn = cx.sb([128, 512], F32, ph)
        eb = cx.sb([128, 512], F32, ph)
        qf = cx.sb([128, DM], F32, ph)
        vao = cx.sb([128, 2 * 1040], BF16, ph)

        for g4 in range(4):
            emit_ln_blocks(cx,
                           lambda i, g4=g4: x2[:, (g4 * 4 + i) * DM:(g4 * 4 + i + 1) * DM],
                           lambda i: xnt[:, i * DM:(i + 1) * DM], sq, stw, 4)
            jobs = []
            for i in range(4):
                ti = g4 * 4 + i
                for c in range(DC):
                    jobs.append((xnt[:, i * DM + c * 128:i * DM + (c + 1) * 128],
                                 xn1[:, c * HALF + ti * 128:c * HALF + (ti + 1) * 128]))
            emit_transpose_block(cx, P, pst, jobs)

        with nc.Block() as blk:
            @blk.sync
            def _(sy):
                sy.wait_ge(cx.gd.h, P["m_wk"])
                for hc in range(DC):
                    cx.dm.inc(sy.dma_start_transpose(
                        wres[:, hc * DM:(hc + 1) * DM],
                        wkbf_d[:, hc * 128:(hc + 1) * 128]), 16)
                sy.wait_ge(cx.dm.h, cx.dm.n)
        with nc.Block() as blk:
            pe0, dv0, ac0 = cx.pe.n, cx.dve.n, cx.act.n
            @blk.tensor
            def _(te):
                for ti in range(NT):
                    if ti >= 3:
                        te.wait_ge(cx.dve.h, dv0 + 6 * (ti - 3) + 2)
                    b0 = (ti % 3) * 2
                    mm = None
                    for hc in range(DC):
                        for hf in range(2):
                            mm = te.matmul(
                                ps[b0 + hf][:],
                                xn1[:, hc * HALF + ti * 128:hc * HALF + (ti + 1) * 128],
                                wres[:, hc * DM + hf * 512:hc * DM + (hf + 1) * 512],
                                start=(hc == 0), stop=(hc == DC - 1))
                    cx.pe.inc(mm)
            @blk.vector
            def _(v):
                for ti in range(NT):
                    b0 = (ti % 3) * 2
                    v.wait_ge(cx.pe.h, pe0 + ti + 1)
                    cx.dve.inc(v.tensor_add(qf[:, 0:512], ps[b0][:], bkb[:, 0:512]))
                    cx.dve.inc(v.tensor_add(qf[:, 512:1024], ps[b0 + 1][:],
                                            bkb[:, 512:1024]))
                    for hf in range(2):
                        sl = slice(hf * 512, (hf + 1) * 512)
                        v.tensor_relu(rb[:], qf[:, sl])
                        cx.dve.inc(v.tensor_scalar_min(mn[:], qf[:, sl], 0.0))
                        v.wait_ge(cx.act.h, ac0 + 2 * ti + hf + 1)
                        cx.dve.inc(v.tensor_add(
                            ktm[:, ti * 1024 + hf * 512:ti * 1024 + (hf + 1) * 512],
                            eb[:], rb[:]))
            @blk.scalar
            def _(s):
                for ti in range(NT):
                    for hf in range(2):
                        s.wait_ge(cx.dve.h, dv0 + 6 * ti + 3 + 2 * hf)
                        cx.act.inc(s.activation(eb[:], mn[:], AF.Exp))

        with nc.Block() as blk:
            @blk.sync
            def _(sy):
                sy.wait_ge(cx.gd.h, P["m_wv"])
                for hc in range(DC):
                    cx.dm.inc(sy.dma_start_transpose(
                        wres[:, hc * DM:(hc + 1) * DM],
                        wvbf_d[:, hc * 128:(hc + 1) * 128]), 16)
                sy.wait_ge(cx.dm.h, cx.dm.n)
            @blk.vector
            def _(v):
                v.memset(vao[:], 1.0)
        def spk(h):
            return (4 + h % 2, ((h // 2) % 2) * 64, (h // 4) * 65)
        with nc.Block() as blk:
            pe0, dv0, dm0 = cx.pe.n, cx.dve.n, cx.dm.n
            vmm_marks = []
            def emit_vmm(te, ti):
                if ti >= 2:
                    te.wait_ge(cx.dve.h, dv0 + 16 * (ti - 2) + 16)
                b0 = (ti % 2) * 2
                mm = None
                for hc in range(DC):
                    for hf in range(2):
                        mm = te.matmul(
                            ps[b0 + hf][:],
                            xn1[:, hc * HALF + ti * 128:hc * HALF + (ti + 1) * 128],
                            wres[:, hc * DM + hf * 512:hc * DM + (hf + 1) * 512],
                            start=(hc == 0), stop=(hc == DC - 1))
                vmm_marks.append(cx.pe.inc(mm))
            def emit_state(te, ti):
                te.wait_ge(cx.dve.h, dv0 + 16 * ti + 16)
                mm = None
                for h in range(H1):
                    bk, rr0, cc0 = spk(h)
                    mm = te.matmul(
                        ps[bk][rr0:rr0 + 64, cc0:cc0 + 65],
                        ktm[:, ti * 1024 + h * 64:ti * 1024 + (h + 1) * 64],
                        vao[:, (ti % 2) * 1040 + h * 65:(ti % 2) * 1040 + (h + 1) * 65],
                        start=(ti == 0 and h < 2), stop=(ti == NT - 1))
                cx.pe.inc(mm)
            @blk.tensor
            def _(te):
                emit_vmm(te, 0)
                for ti in range(1, NT):
                    emit_vmm(te, ti)
                    emit_state(te, ti - 1)
                emit_state(te, NT - 1)
            @blk.vector
            def _(v):
                for ti in range(NT):
                    b0 = (ti % 2) * 2
                    v.wait_ge(cx.pe.h, vmm_marks[ti])
                    if ti >= 2:
                        v.wait_ge(cx.dm.h, dm0 + 16 * (ti - 1))
                    for h in range(H1):
                        hf, off = divmod(h * 64, 512)
                        cx.dve.inc(v.tensor_add(
                            vao[:, (ti % 2) * 1040 + h * 65:(ti % 2) * 1040 + h * 65 + 64],
                            ps[b0 + hf][:, off:off + 64],
                            bvb[:, h * 64:(h + 1) * 64]))
            @blk.sync
            def _(sy):
                for ti in range(NT):
                    sy.wait_ge(cx.dve.h, dv0 + 16 * ti + 16)
                    cx.dm.inc(sy.dma_start(va_d[ti * 128:(ti + 1) * 128, :],
                                           vao[:, (ti % 2) * 1040:(ti % 2 + 1) * 1040]), 16)
        with nc.Block() as blk:
            @blk.vector
            def _(v):
                for h in range(H1):
                    bk, rr0, cc0 = spk(h)
                    base, co = (h % 2) * 64, (h // 2) * 65
                    v.tensor_copy(Srep[base:base + 64, co:co + 65],
                                  ps[bk][rr0:rr0 + 64, cc0:cc0 + 65])


def emit_l1_ffn(cx, t, P, ps, pst):
    nc = cx.nc
    x2 = P["x2"]
    xn1 = P["xn"]
    gwT, gbb, wl1 = P["gwT"], P["gbb"], P["wl1"]
    with ExitStack() as ph:
        ebs = cx.sb([8, DM], F32, ph)
        gt = cx.sb([128, NT * 8], F32, ph)
        ms1 = cx.sb([128, NT * 8], F32, ph)
        ms2 = cx.sb([128, NT * 8], F32, ph)
        a1 = cx.sb([128, NT * 8], F32, ph)
        a2 = cx.sb([128, NT * 8], F32, ph)
        oh1 = cx.sb([128, NT * 8], F32, ph)
        sc8 = cx.sb([128, NT * 8], F32, ph)
        wT8 = cx.sb([8, NT * 128], F32, ph)
        ewT = cx.sb([128, DC * DM], BF16, ph)
        wtmp = cx.sb([128, 2 * DM], BF16, ph)

        with nc.Block() as blk:
            @blk.sync
            def _(sy):
                cx.dm.inc(sy.dma_start(ebs[:], t["eb1"]), 16)
                sy.wait_ge(cx.dm.h, cx.dm.n)
        with nc.Block() as blk:
            @blk.tensor
            def _(te):
                for tt in range(NT):
                    for c in range(DC):
                        te.matmul(ps[0][:, tt * 8:(tt + 1) * 8],
                                  xn1[:, c * HALF + tt * 128:c * HALF + (tt + 1) * 128],
                                  gwT[:, c * 8:(c + 1) * 8],
                                  start=(c == 0 and tt == 0), stop=(c == DC - 1))
        with nc.Block() as blk:
            dv0 = cx.dve.n
            @blk.vector
            def _(v):
                for tt in range(NT):
                    sl = slice(tt * 8, (tt + 1) * 8)
                    m = cx.dve.inc(v.tensor_add(gt[:, sl], ps[0][:, sl], gbb[:]))
                    v.wait_ge(cx.dve.h, m)
                    m = cx.dve.inc(v.reduce_max(ms1[:, tt * 8:tt * 8 + 1], gt[:, sl],
                                                axis=AX, negate=True))
                    v.wait_ge(cx.dve.h, m)
            @blk.scalar
            def _(s):
                for tt in range(NT):
                    sl = slice(tt * 8, (tt + 1) * 8)
                    s.wait_ge(cx.dve.h, dv0 + 2 * (tt + 1))
                    s.activation(gt[:, sl], gt[:, sl], AF.Exp,
                                 bias=ms1[:, tt * 8:tt * 8 + 1])
        with nc.Block() as blk:
            @blk.vector
            def _(v):
                for tt in range(NT):
                    sl = slice(tt * 8, (tt + 1) * 8)
                    m = cx.dve.inc(v.reduce_sum(ms2[:, tt * 8:tt * 8 + 1], gt[:, sl],
                                                axis=AX))
                    v.wait_ge(cx.dve.h, m)
                    m = cx.dve.inc(v.reciprocal(sc8[:, tt * 8:tt * 8 + 1],
                                                ms2[:, tt * 8:tt * 8 + 1]))
                    v.wait_ge(cx.dve.h, m)
                    v.tensor_scalar(gt[:, sl], gt[:, sl], sc8[:, tt * 8:tt * 8 + 1],
                                    None, op0=ALU.mult)
        with nc.Block() as blk:
            @blk.vector
            def _(v):
                for tt in range(NT):
                    sl = slice(tt * 8, (tt + 1) * 8)
                    v.memset(ms1[:, sl], -1e30)
                    v.memset(ms2[:, sl], -1e30)
                    m = cx.dve.inc(v.reduce_max(ms1[:, tt * 8:tt * 8 + 1], gt[:, sl],
                                                axis=AX))
                    v.wait_ge(cx.dve.h, m)
                    v.match_replace(out=a1[:, sl], in_to_replace=ms1[:, sl],
                                    in_values=gt[:, sl], imm_value=-1e30)
        with nc.Block() as blk:
            @blk.vector
            def _(v):
                for tt in range(NT):
                    sl = slice(tt * 8, (tt + 1) * 8)
                    m = cx.dve.inc(v.tensor_sub(oh1[:, sl], gt[:, sl], a1[:, sl]))
                    v.wait_ge(cx.dve.h, m)
                    v.tensor_scalar(oh1[:, sl], oh1[:, sl], 1e-29, 1.0,
                                    op0=ALU.mult, op1=ALU.min)
                    m = cx.dve.inc(v.reduce_max(ms2[:, tt * 8:tt * 8 + 1], a1[:, sl],
                                                axis=AX))
                    v.wait_ge(cx.dve.h, m)
                    v.match_replace(out=a2[:, sl], in_to_replace=ms2[:, sl],
                                    in_values=a1[:, sl], imm_value=-1e30)
        with nc.Block() as blk:
            @blk.vector
            def _(v):
                for tt in range(NT):
                    sl = slice(tt * 8, (tt + 1) * 8)
                    m = cx.dve.inc(v.tensor_sub(wl1[:, sl], a1[:, sl], a2[:, sl]))
                    v.wait_ge(cx.dve.h, m)
                    m = cx.dve.inc(v.tensor_scalar(wl1[:, sl], wl1[:, sl], 1e-29, 1.0,
                                                   op0=ALU.mult, op1=ALU.min))
                    v.wait_ge(cx.dve.h, m)
                    m = cx.dve.inc(v.tensor_add(wl1[:, sl], wl1[:, sl], oh1[:, sl]))
                    v.wait_ge(cx.dve.h, m)
                    v.tensor_mul(wl1[:, sl], wl1[:, sl], gt[:, sl])
        with nc.Block() as blk:
            pe0, dv0 = cx.pe.n, cx.dve.n
            @blk.tensor
            def _(te):
                for tt in range(NT):
                    if tt >= 2:
                        te.wait_ge(cx.dve.h, dv0 + (tt - 2) + 1)
                    cx.pe.inc(te.transpose(ps[tt % 2][0:8, 0:128],
                                           wl1[:, tt * 8:(tt + 1) * 8], P["idf"][:]))
            @blk.vector
            def _(v):
                for tt in range(NT):
                    v.wait_ge(cx.pe.h, pe0 + tt + 1)
                    cx.dve.inc(v.tensor_copy(wT8[0:8, tt * 128:(tt + 1) * 128],
                                             ps[tt % 2][0:8, 0:128]))
        with nc.Block() as blk:
            pe0, dv0 = cx.pe.n, cx.dve.n
            @blk.tensor
            def _(te):
                g = 0
                for tt in range(NT):
                    for hf in range(2):
                        if g >= 2:
                            te.wait_ge(cx.dve.h, dv0 + (g - 2) + 1)
                        cx.pe.inc(te.matmul(
                            ps[4 + g % 2][:], wT8[0:8, tt * 128:(tt + 1) * 128],
                            ebs[0:8, hf * 512:(hf + 1) * 512],
                            start=True, stop=True))
                        g += 1
            @blk.vector
            def _(v):
                g = 0
                for tt in range(NT):
                    for hf in range(2):
                        v.wait_ge(cx.pe.h, pe0 + g + 1)
                        xsl = x2[:, tt * DM + hf * 512:tt * DM + (hf + 1) * 512]
                        cx.dve.inc(v.tensor_add(xsl, xsl, ps[4 + g % 2][:]))
                        g += 1
        ewbf_d = P["ewbf_d"]
        for e in range(E1):
            with nc.Block() as blk:
                @blk.sync
                def _(sy):
                    if e == 0 and "m_ew" in P:
                        sy.wait_ge(cx.gd.h, P["m_ew"])
                    for hc in range(DC):
                        cx.dm.inc(sy.dma_start_transpose(
                            ewT[:, hc * DM:(hc + 1) * DM],
                            ewbf_d[e * DM:(e + 1) * DM, hc * 128:(hc + 1) * 128]), 16)
                    sy.wait_ge(cx.dm.h, cx.dm.n)
            with nc.Block() as blk:
                pe0, dv0 = cx.pe.n, cx.dve.n
                @blk.tensor
                def _(te):
                    g = 0
                    for tt in range(NT):
                        for hf in range(2):
                            if g >= 2:
                                te.wait_ge(cx.dve.h, dv0 + (g - 2) + 1)
                            mm = None
                            for hc in range(DC):
                                mm = te.matmul(
                                    ps[g % 2][:],
                                    xn1[:, hc * HALF + tt * 128:hc * HALF + (tt + 1) * 128],
                                    ewT[:, hc * DM + hf * 512:hc * DM + (hf + 1) * 512],
                                    start=(hc == 0), stop=(hc == DC - 1))
                            cx.pe.inc(mm)
                            g += 1
                @blk.vector
                def _(v):
                    g = 0
                    for tt in range(NT):
                        for hf in range(2):
                            v.wait_ge(cx.pe.h, pe0 + g + 1)
                            xsl = x2[:, tt * DM + hf * 512:tt * DM + (hf + 1) * 512]
                            cx.dve.inc(v.scalar_tensor_tensor(
                                out=xsl, in0=ps[g % 2][:],
                                scalar=wl1[:, tt * 8 + e:tt * 8 + e + 1],
                                in1=xsl, op0=ALU.mult, op1=ALU.add))
                            g += 1


def declare_l1_inputs(nc):
    t = {}
    t["x1r"] = nc.dram_tensor("x1r", [S_FULL, DM], F32, kind="ExternalInput").ap()
    for n in ("wq1", "wk1", "wv1", "wo1"):
        t[n] = nc.dram_tensor(n, [DM, DM], F32, kind="ExternalInput").ap()
    for n in ("bq1", "bk1", "bv1", "bo1"):
        t[n] = nc.dram_tensor(n, [DM], F32, kind="ExternalInput").ap()
    t["gw1"] = nc.dram_tensor("gw1", [E1, DM], F32, kind="ExternalInput").ap()
    t["gb1"] = nc.dram_tensor("gb1", [E1], F32, kind="ExternalInput").ap()
    t["ew1"] = nc.dram_tensor("ew1", [E1, DM, DM], F32, kind="ExternalInput").ap()
    t["eb1"] = nc.dram_tensor("eb1", [E1, DM], F32, kind="ExternalInput").ap()
    t["masku"] = nc.dram_tensor("masku", [CH, CH], F32, kind="ExternalInput").ap()
    t["is2nd"] = nc.dram_tensor("is2nd", [128, 1], F32, kind="ExternalInput").ap()
    return t


def build_l1_program():
    nc = bass.Bass("TRN2", target_bir_lowering=False, debug=False)
    es = ExitStack()
    cx = Cx(nc, es)
    t = declare_l1_inputs(nc)
    y = nc.dram_tensor("yout", [HALF, DM], F32, kind="ExternalOutput").ap()
    P = {}
    P["idf"] = cx.sb([128, 128], F32)
    P["idb"] = cx.sb([128, 128], BF16)
    P["ones_row"] = cx.sb([1, 128], F32)
    P["ones_col"] = cx.sb([128, 1], BF16)
    P["xn"] = cx.sb([128, DC * HALF], BF16)
    ps = [es.enter_context(nc.psum_tensor(f"ps{i}", [128, 512], F32)) for i in range(6)]
    pst = [es.enter_context(nc.psum_tensor("pstA", [128, 1024], BF16)),
           es.enter_context(nc.psum_tensor("pstB", [128, 1024], BF16))]
    emit_identity_ones(nc, P, cx)
    emit_l1(cx, t, P, ps, pst, fused=False)
    with nc.Block() as blk:
        @blk.sync
        def _(sy):
            for tt in range(NT):
                cx.dm.inc(sy.dma_start(y[tt * 128:(tt + 1) * 128, :],
                                       P["x2"][:, tt * DM:(tt + 1) * DM]), 16)
            sy.wait_ge(cx.dm.h, cx.dm.n)
    es.close()
    return nc


def l1_in_map(core, inputs, x1_full):
    b, par = core // 2, core % 2
    x = np.asarray(x1_full[b])
    xr = np.concatenate([x[par * HALF:(par + 1) * HALF],
                         x[(1 - par) * HALF:(2 - par) * HALF]], axis=0)
    d = {"x1r": np.ascontiguousarray(xr),
         "wq1": inputs["L1_attn_Wq"], "bq1": inputs["L1_attn_bq"],
         "wk1": inputs["L1_attn_Wk"], "bk1": inputs["L1_attn_bk"],
         "wv1": inputs["L1_attn_Wv"], "bv1": inputs["L1_attn_bv"],
         "wo1": inputs["L1_attn_Wo"], "bo1": inputs["L1_attn_bo"],
         "gw1": inputs["L1_ffn_gate_w"], "gb1": inputs["L1_ffn_gate_b"],
         "ew1": inputs["L1_ffn_expert_w"], "eb1": inputs["L1_ffn_expert_b"],
         "masku": np.triu(np.ones((CH, CH), np.float32)),
         "is2nd": np.full((128, 1), np.float32(par))}
    return {k: np.ascontiguousarray(np.asarray(v, dtype=np.float32)) for k, v in d.items()}


def emit_handshake_send(cx, t, P, Srep, hses):
    nc = cx.nc
    P["sh_state"] = nc.dram_tensor("sh_state", [2 * 1024, 65], F32,
                                   kind="Internal", addr_space="Shared").ap()
    P["sh_flag"] = nc.dram_tensor("sh_flag", [32, 4], I32, kind="Internal",
                                  addr_space="Shared").ap()
    widx = cx.sb([128, 8], I32, hses)
    ridx = cx.sb([128, 8], I32, hses)
    wfi = cx.sb([16, 1], I32, hses)
    non = cx.sb([16, 4], I32, hses)
    fl0 = cx.sb([16, 4], I32, hses)
    fl1 = cx.sb([16, 4], I32, hses)
    P["hs"] = (widx, ridx, wfi, non, fl0, fl1)
    with nc.Block() as blk:
        @blk.sync
        def _(sy):
            cx.dm.inc(sy.dma_start(widx[:], t["widx"]), 16)
            cx.dm.inc(sy.dma_start(ridx[:], t["ridx"]), 16)
            cx.dm.inc(sy.dma_start(wfi[:], t["wfidx"]), 16)
            cx.dm.inc(sy.dma_start(non[:], t["nonce"]), 16)
            sy.wait_ge(cx.dm.h, cx.dm.n)
    with nc.Block() as blk:
        @blk.gpsimd
        def _(g):
            g0 = cx.gd.n
            for c in range(DC):
                cx.gd.inc(g.indirect_dma_start(
                    out=P["sh_state"],
                    out_offset=bass.IndirectOffsetOnAxis(ap=widx[:, c:c + 1], axis=0),
                    in_=Srep[:, c * 65:(c + 1) * 65], in_offset=None), 16)
            g.wait_ge(cx.gd.h, g0 + DC * 16)
            cx.gd.inc(g.indirect_dma_start(
                out=P["sh_flag"],
                out_offset=bass.IndirectOffsetOnAxis(ap=wfi[:], axis=0),
                in_=non[:], in_offset=None), 16)
            g.wait_ge(cx.gd.h, cx.gd.n)


def emit_handshake_recv(cx, t, P, Srep):
    nc = cx.nc
    widx, ridx, wfi, non, fl0, fl1 = P["hs"]
    with nc.Block() as blk:
        @blk.gpsimd
        def _(g):
            magic = g.value_load(non[0:1, 0:1])
            with g.register("cnt") as cnt, g.register("diff") as diff:
                g.reg_mov(cnt, 0)
                g.reg_mov(diff, 1)
                with g.While(diff):
                    g.reg_add(cnt, cnt, 32)
                    g.dma_start(fl0[:], P["sh_flag"][0:16, :]).then_inc(cx.hsp.h, 16)
                    g.dma_start(fl1[:], P["sh_flag"][16:32, :]).then_inc(cx.hsp.h, 16)
                    g.wait_ge(cx.hsp.h, cnt)
                    f0 = g.value_load(fl0[0:1, 0:1])
                    f1 = g.value_load(fl1[0:1, 0:1])
                    with g.register("d0") as d0, g.register("d1") as d1:
                        g.reg_alu(d0, f0, magic, ALU.bitwise_xor)
                        g.reg_alu(d1, f1, magic, ALU.bitwise_xor)
                        g.reg_alu(diff, d0, d1, ALU.bitwise_or)
            for c in range(DC):
                cx.aux.inc(g.indirect_dma_start(
                    out=Srep[:, c * 65:(c + 1) * 65], out_offset=None,
                    in_=P["sh_state"],
                    in_offset=bass.IndirectOffsetOnAxis(ap=ridx[:, c:c + 1], axis=0)), 16)
            g.wait_ge(cx.aux.h, cx.aux.n)


def build_fused_program():
    nc = bass.Bass("TRN2", target_bir_lowering=False, debug=False)
    es = ExitStack()
    cx = Cx(nc, es)
    t = declare_l0_inputs(nc)
    t.update(declare_l1_inputs_fused(nc))
    y = nc.dram_tensor("yout", [HALF, DM], F32, kind="ExternalOutput").ap()
    P = {}
    P["idf"] = cx.sb([128, 128], F32)
    P["idb"] = cx.sb([128, 128], BF16)
    P["ones_row"] = cx.sb([1, 128], F32)
    P["ones_col"] = cx.sb([128, 1], BF16)
    P["xn"] = cx.sb([128, DC * HALF], BF16)
    P["wl0"] = cx.sb([128, NT * E0], F32)
    ps = [es.enter_context(nc.psum_tensor(f"ps{i}", [128, 512], F32)) for i in range(6)]
    pst = [es.enter_context(nc.psum_tensor("pstA", [128, 1024], BF16)),
           es.enter_context(nc.psum_tensor("pstB", [128, 1024], BF16))]
    emit_identity_ones(nc, P, cx)
    emit_l0(cx, t, P, ps, pst)
    P["x2"] = P["x1"]
    emit_l1(cx, t, P, ps, pst, fused=True)
    with nc.Block() as blk:
        @blk.sync
        def _(sy):
            for tt in range(NT):
                cx.dm.inc(sy.dma_start(y[tt * 128:(tt + 1) * 128, :],
                                       P["x2"][:, tt * DM:(tt + 1) * DM]), 16)
            sy.wait_ge(cx.dm.h, cx.dm.n)
    es.close()
    return nc


def declare_l1_inputs_fused(nc):
    t = {}
    for n in ("wq1", "wk1", "wv1", "wo1"):
        t[n] = nc.dram_tensor(n, [DM, DM], F32, kind="ExternalInput").ap()
    for n in ("bq1", "bk1", "bv1", "bo1"):
        t[n] = nc.dram_tensor(n, [DM], F32, kind="ExternalInput").ap()
    t["gw1"] = nc.dram_tensor("gw1", [E1, DM], F32, kind="ExternalInput").ap()
    t["gb1"] = nc.dram_tensor("gb1", [E1], F32, kind="ExternalInput").ap()
    t["ew1"] = nc.dram_tensor("ew1", [E1, DM, DM], F32, kind="ExternalInput").ap()
    t["eb1"] = nc.dram_tensor("eb1", [E1, DM], F32, kind="ExternalInput").ap()
    t["masku"] = nc.dram_tensor("masku", [CH, CH], F32, kind="ExternalInput").ap()
    t["is2nd"] = nc.dram_tensor("is2nd", [128, 1], F32, kind="ExternalInput").ap()
    t["widx"] = nc.dram_tensor("widx", [128, 8], I32, kind="ExternalInput").ap()
    t["ridx"] = nc.dram_tensor("ridx", [128, 8], I32, kind="ExternalInput").ap()
    t["wfidx"] = nc.dram_tensor("wfidx", [16, 1], I32, kind="ExternalInput").ap()
    t["nonce"] = nc.dram_tensor("nonce", [16, 4], I32, kind="ExternalInput").ap()
    return t


def fused_in_map(core, inputs, nonce):
    b, par = core // 2, core % 2
    d = l0_in_map(core, inputs)
    d.update({
        "wq1": inputs["L1_attn_Wq"], "bq1": inputs["L1_attn_bq"],
        "wk1": inputs["L1_attn_Wk"], "bk1": inputs["L1_attn_bk"],
        "wv1": inputs["L1_attn_Wv"], "bv1": inputs["L1_attn_bv"],
        "wo1": inputs["L1_attn_Wo"], "bo1": inputs["L1_attn_bo"],
        "gw1": inputs["L1_ffn_gate_w"], "gb1": inputs["L1_ffn_gate_b"],
        "ew1": inputs["L1_ffn_expert_w"], "eb1": inputs["L1_ffn_expert_b"],
        "masku": np.triu(np.ones((CH, CH), np.float32)),
        "is2nd": np.full((128, 1), np.float32(par)),
        "widx": (par * 1024 + np.arange(1024)).astype(np.int32).reshape(8, 128).T.copy(),
        "ridx": ((1 - par) * 1024 + np.arange(1024)).astype(np.int32).reshape(8, 128).T.copy(),
        "wfidx": (par * 16 + np.arange(16)).astype(np.int32).reshape(16, 1),
        "nonce": np.full((16, 4), nonce, np.int32),
    })
    return {k: np.ascontiguousarray(np.asarray(v, dtype=v.dtype if hasattr(v, "dtype") and np.issubdtype(np.asarray(v).dtype, np.integer) else np.float32)) for k, v in d.items()}


def kernel_fused(**inputs):
    import random
    nonce = random.randint(1, 2 ** 30)
    nc = build_fused_program()
    in_maps = [fused_in_map(c, inputs, nonce) for c in range(NC)]
    res = run_bass_kernel_spmd(nc, in_maps, core_ids=list(range(NC)))
    out = np.empty((NB, S_FULL, DM), np.float32)
    for c in range(NC):
        b, par = c // 2, c % 2
        out[b, par * HALF:(par + 1) * HALF] = res.results[c]["yout"]
    return out


# version 24
# speedup vs baseline: 1.1360x; 1.1360x over previous
import numpy as np
from contextlib import ExitStack

import concourse.bass as bass
import concourse.mybir as mybir
from concourse.bass_utils import run_bass_kernel_spmd

F32 = mybir.dt.float32
BF16 = mybir.dt.bfloat16
I32 = mybir.dt.int32
AF = mybir.ActivationFunctionType
ALU = mybir.AluOpType
AX = mybir.AxisListType.X

DM = 1024
S_FULL = 4096
HALF = 2048
NB = 4
NC = 8
H0, KV0, D0, RD = 8, 2, 256, 64
E0, I0 = 8, 2048
H1, D1, CH = 16, 64, 128
E1 = 8
DC = DM // 128
NT = HALF // 128
LNR = 1.0 / DM
DBG = False


class Sem:
    def __init__(self, nc, name):
        self.h = nc.alloc_semaphore(name)
        self.n = 0

    def inc(self, instr, k=1):
        instr.then_inc(self.h, k)
        self.n += k
        return self.n


class Cx:
    def __init__(self, nc, es):
        self.nc = nc
        self.es = es
        self.dm = Sem(nc, "dm")
        self.gd = Sem(nc, "gd")
        self.pe = Sem(nc, "pe")
        self.act = Sem(nc, "act")
        self.dve = Sem(nc, "dve")
        self.aux = Sem(nc, "aux")
        self.hsp = Sem(nc, "hsp")

    def sb(self, shape, dtype, es=None, side=None):
        self.nsb = getattr(self, "nsb", 0) + 1
        return (es or self.es).enter_context(
            self.nc.sbuf_tensor(f"sb{self.nsb}", shape, dtype, side=side))



def emit_identity_ones(nc, P, cx):
    with nc.Block() as blk:
        @blk.gpsimd
        def _(g):
            g.memset(P["idf"][:], 0.0)
            m = cx.aux.inc(g.memset(P["idb"][:], 0.0))
            g.wait_ge(cx.aux.h, m)
            g.affine_select(P["idf"][:], P["idf"][:], pattern=[[-1, 128]],
                            compare_op=ALU.not_equal, fill=1.0, base=0,
                            channel_multiplier=1)
            g.affine_select(P["idb"][:], P["idb"][:], pattern=[[-1, 128]],
                            compare_op=ALU.not_equal, fill=1.0, base=0,
                            channel_multiplier=1)
        @blk.vector
        def _(v):
            v.memset(P["ones_row"][:], 1.0)
            v.memset(P["ones_col"][:], 1.0)


def emit_ln_blocks(cx, x_tiles, xn_tiles, sq, stw, ntiles):
    nc = cx.nc
    with nc.Block() as blk:
        @blk.vector
        def _(v):
            for i in range(ntiles):
                stv = stw[:, i * 8:(i + 1) * 8]
                v.reduce_sum(stv[:, 0:1], x_tiles(i), axis=AX)
                v.tensor_mul(sq[:], x_tiles(i), x_tiles(i))
                v.reduce_sum(stv[:, 1:2], sq[:], axis=AX)
                m = cx.dve.inc(v.tensor_scalar_mul(stv[:, 2:3], stv[:, 0:1], LNR))
                v.wait_ge(cx.dve.h, m)
                m = cx.dve.inc(v.tensor_scalar(stv[:, 3:4], stv[:, 2:3], stv[:, 2:3],
                                               1e-5, op0=ALU.mult, op1=ALU.subtract))
                v.wait_ge(cx.dve.h, m)
                v.tensor_scalar(stv[:, 4:5], stv[:, 1:2], LNR, stv[:, 3:4],
                                op0=ALU.mult, op1=ALU.subtract)
    with nc.Block() as blk:
        @blk.scalar
        def _(s):
            for i in range(ntiles):
                stv = stw[:, i * 8:(i + 1) * 8]
                s.activation(stv[:, 5:6], stv[:, 4:5], AF.Ln)
                s.activation(stv[:, 6:7], stv[:, 5:6], AF.Exp, scale=-0.5)
    with nc.Block() as blk:
        @blk.vector
        def _(v):
            for i in range(ntiles):
                stv = stw[:, i * 8:(i + 1) * 8]
                v.tensor_scalar(xn_tiles(i), x_tiles(i), stv[:, 2:3], stv[:, 6:7],
                                op0=ALU.subtract, op1=ALU.mult)


def emit_transpose_block(cx, P, pst, jobs, ident_key="idb"):
    nc = cx.nc
    groups = [jobs[i:i + 4] for i in range(0, len(jobs), 4)]
    with nc.Block() as blk:
        pe0, dv0 = cx.pe.n, cx.dve.n
        @blk.tensor
        def _(te):
            done = 0
            for g, grp in enumerate(groups):
                if g >= 2:
                    te.wait_ge(cx.dve.h, dv0 + sum(len(x) for x in groups[:g - 1]))
                bk = pst[g % 2]
                for j, (src, _dst) in enumerate(grp):
                    cx.pe.inc(te.transpose(bk[:, j * 128:(j + 1) * 128], src,
                                           P[ident_key][:]))
                done += len(grp)
        @blk.vector
        def _(v):
            done = 0
            for g, grp in enumerate(groups):
                done += len(grp)
                v.wait_ge(cx.pe.h, pe0 + done)
                bk = pst[g % 2]
                for j, (_src, dst) in enumerate(grp):
                    cx.dve.inc(v.tensor_copy(dst, bk[:, j * 128:(j + 1) * 128]))


def emit_rope_block(cx, ps_list, groups, cs_view, sn_view, rr, pass_scale):
    raise NotImplementedError



def emit_l0(cx, t, P, ps, pst):
    nc = cx.nc

    aoes = ExitStack()
    ao = None

    def aosl(r, q5):
        return ao[:, r * HALF + q5 * 512:r * HALF + (q5 + 1) * 512]

    with ExitStack() as kves:
        kfm = cx.sb([128, 4 * S_FULL], BF16, kves)
        vtm = cx.sb([128, 32 * 512], BF16, kves)
        cq = cx.sb([64, 2 * HALF], F32, kves)

        with ExitStack() as ph:
            cs = cx.sb([64, 2 * 1024], F32, ph)
            xin = cx.sb([128, 8 * DM], F32, ph)
            xnt = cx.sb([128, 4 * DM], BF16, ph)
            xf = cx.sb([128, DC * 512], BF16, ph)
            sq = cx.sb([128, DM], F32, ph)
            stw = cx.sb([128, 4 * 8], F32, ph)
            wkv = cx.sb([128, 2 * 4096], BF16, ph)
            rr = cx.sb([64, 3 * 512], F32, ph)

            x_mk = {}
            def x_load(sy, s5_):
                b = (s5_ % 2) * 4 * DM
                for i in range(4):
                    cx.dm.inc(sy.dma_start(
                        xin[:, b + i * DM:b + (i + 1) * DM],
                        t["x0"][s5_ * 512 + i * 128:s5_ * 512 + (i + 1) * 128, :]), 16)
                cb = (s5_ % 2) * 1024
                cx.dm.inc(sy.dma_start(cs[:, cb:cb + 512],
                                       t["cosk"][:, s5_ * 512:(s5_ + 1) * 512]), 16)
                cx.dm.inc(sy.dma_start(cs[:, cb + 512:cb + 1024],
                                       t["sink"][:, s5_ * 512:(s5_ + 1) * 512]), 16)
                x_mk[s5_] = cx.dm.n

            with nc.Block() as blk:
                @blk.gpsimd
                def _(g):
                    for c in range(DC):
                        cx.gd.inc(g.dma_start(wkv[:, c * 512:(c + 1) * 512],
                                              t["wk0"][c * 128:(c + 1) * 128, :]), 16)
                        cx.gd.inc(g.dma_start(wkv[:, 4096 + c * 512:4096 + (c + 1) * 512],
                                              t["wv0"][c * 128:(c + 1) * 128, :]), 16)
                    g.wait_ge(cx.gd.h, cx.gd.n)
                @blk.sync
                def _(sy):
                    cx.dm.inc(sy.dma_start(cq[:, 0:HALF], t["cosq"]), 16)
                    cx.dm.inc(sy.dma_start(cq[:, HALF:], t["sinq"]), 16)
                    mcq = cx.dm.n
                    x_load(sy, 0)
                    sy.wait_ge(cx.dm.h, mcq)

            for s5 in range(8):
                own = s5 < 4
                with nc.Block() as blk:
                    @blk.sync
                    def _(sy):
                        sy.wait_ge(cx.dm.h, x_mk[s5])
                        if s5 + 1 < 8:
                            x_load(sy, s5 + 1)
                xb = (s5 % 2) * 4 * DM
                emit_ln_blocks(cx, lambda i: xin[:, xb + i * DM:xb + (i + 1) * DM],
                               lambda i: xnt[:, i * DM:(i + 1) * DM], sq, stw, 4)
                jobs = []
                for i in range(4):
                    for c in range(DC):
                        src = xnt[:, i * DM + c * 128:i * DM + (c + 1) * 128]
                        if own:
                            dst = P["xn"][:, c * HALF + s5 * 512 + i * 128:
                                          c * HALF + s5 * 512 + (i + 1) * 128]
                        else:
                            dst = xf[:, c * 512 + i * 128:c * 512 + (i + 1) * 128]
                        jobs.append((src, dst))
                emit_transpose_block(cx, P, pst, jobs)

                if own:
                    def xc(c, s5=s5):
                        return P["xn"][:, c * HALF + s5 * 512:c * HALF + (s5 + 1) * 512]
                else:
                    def xc(c):
                        return xf[:, c * 512:(c + 1) * 512]

                with nc.Block() as blk:
                    pe0, ac0, dv0 = cx.pe.n, cx.act.n, cx.dve.n
                    @blk.tensor
                    def _(te):
                        for dc in range(4):
                            mm = None
                            for c in range(DC):
                                mm = te.matmul(
                                    ps[dc][:],
                                    wkv[:, c * 512 + dc * 128:c * 512 + (dc + 1) * 128],
                                    xc(c), start=(c == 0), stop=(c == DC - 1))
                            cx.pe.inc(mm)
                    @blk.scalar
                    def _(s):
                        nrope = 0
                        for dc in range(4):
                            s.wait_ge(cx.pe.h, pe0 + dc + 1)
                            kd = kfm[:, dc * S_FULL + s5 * 512:dc * S_FULL + (s5 + 1) * 512]
                            if dc % 2 == 0:
                                if nrope >= 1:
                                    s.wait_ge(cx.dve.h, dv0 + nrope)
                                s.activation(rr[0:32, 0:512], ps[dc][32:64, :],
                                             AF.Copy, scale=-1.0)
                                cx.act.inc(s.activation(rr[32:64, 0:512],
                                                        ps[dc][0:32, :], AF.Copy))
                                s.activation(kd[64:128, :], ps[dc][64:128, :], AF.Copy)
                                nrope += 1
                            else:
                                s.activation(kd[0:64, :], ps[dc][0:64, :], AF.Copy)
                                s.activation(kd[64:128, :], ps[dc][64:128, :], AF.Copy)
                    @blk.vector
                    def _(v):
                        cb = (s5 % 2) * 1024
                        csv = cs[:, cb:cb + 512]
                        snv = cs[:, cb + 512:cb + 1024]
                        for j, dc in enumerate([0, 2]):
                            v.wait_ge(cx.act.h, ac0 + j + 1)
                            kd = kfm[:, dc * S_FULL + s5 * 512:dc * S_FULL + (s5 + 1) * 512]
                            v.tensor_mul(rr[:, 512:1024], ps[dc][0:64, :], csv)
                            v.tensor_mul(rr[:, 1024:1536], rr[0:64, 0:512], snv)
                            cx.dve.inc(v.tensor_add(kd[0:64, :], rr[:, 512:1024],
                                                    rr[:, 1024:1536]))

                with nc.Block() as blk:
                    pe0 = cx.pe.n
                    @blk.tensor
                    def _(te):
                        for i in range(4):
                            mm = None
                            for c in range(DC):
                                mm = te.matmul(
                                    ps[i][:], xc(c)[:, i * 128:(i + 1) * 128],
                                    wkv[:, 4096 + c * 512:4096 + (c + 1) * 512],
                                    start=(c == 0), stop=(c == DC - 1))
                            cx.pe.inc(mm)
                    @blk.scalar
                    def _(s):
                        for i in range(4):
                            s.wait_ge(cx.pe.h, pe0 + i + 1)
                            r = s5 * 4 + i
                            s.activation(vtm[:, r * 512:(r + 1) * 512], ps[i][:], AF.Copy)

        if DBG:
            with nc.Block() as blk:
                @blk.gpsimd
                def _(g):
                    cx.gd.inc(g.dma_start(P["dbg_xn"], P["xn"][:]), 16)
                    cx.gd.inc(g.dma_start(P["dbg_kfm"], kfm[:]), 16)
                    cx.gd.inc(g.dma_start(P["dbg_vtm"], vtm[:]), 16)
                    g.wait_ge(cx.gd.h, cx.gd.n)
        ao = cx.sb([128, 16 * HALF], BF16, aoes, side="right")
        with ExitStack() as ph:
            qw = cx.sb([128, 8 * 256], BF16, ph)
            qh = cx.sb([128, 2 * HALF], BF16, ph)
            rr = cx.sb([64, 3 * 512], F32, ph)
            expt = cx.sb([128, 4 * 512], BF16, ph)
            rden = cx.sb([1, 512], F32, ph)
            rdbc = cx.sb([128, 512], F32, ph)

            for h in range(H0):
                kv = h // 4
                with nc.Block() as blk:
                    @blk.gpsimd
                    def _(g):
                        for c in range(DC):
                            cx.gd.inc(g.dma_start(
                                qw[:, c * 256:(c + 1) * 256],
                                t["wq0"][c * 128:(c + 1) * 128, h * 256:(h + 1) * 256]), 16)
                        g.wait_ge(cx.gd.h, cx.gd.n)
                for oc in range(2):
                    with nc.Block() as blk:
                        pe0, ac0, dv0 = cx.pe.n, cx.act.n, cx.dve.n
                        @blk.tensor
                        def _(te):
                            for q5 in range(4):
                                mm = None
                                for c in range(DC):
                                    mm = te.matmul(
                                        ps[q5][:],
                                        qw[:, c * 256 + oc * 128:c * 256 + (oc + 1) * 128],
                                        P["xn"][:, c * HALF + q5 * 512:c * HALF + (q5 + 1) * 512],
                                        start=(c == 0), stop=(c == DC - 1))
                                cx.pe.inc(mm)
                        @blk.scalar
                        def _(s):
                            for q5 in range(4):
                                s.wait_ge(cx.pe.h, pe0 + q5 + 1)
                                qd = qh[:, oc * HALF + q5 * 512:oc * HALF + (q5 + 1) * 512]
                                if oc == 0:
                                    if q5 >= 1:
                                        s.wait_ge(cx.dve.h, dv0 + q5)
                                    s.activation(rr[0:32, 0:512], ps[q5][32:64, :],
                                                 AF.Copy, scale=-1.0)
                                    cx.act.inc(s.activation(rr[32:64, 0:512],
                                                            ps[q5][0:32, :], AF.Copy))
                                    s.activation(qd[64:128, :], ps[q5][64:128, :],
                                                 AF.Copy, scale=0.0625)
                                else:
                                    s.activation(qd[0:64, :], ps[q5][0:64, :],
                                                 AF.Copy, scale=0.0625)
                                    s.activation(qd[64:128, :], ps[q5][64:128, :],
                                                 AF.Copy, scale=0.0625)
                        if oc == 0:
                            @blk.vector
                            def _(v):
                                for q5 in range(4):
                                    v.wait_ge(cx.act.h, ac0 + q5 + 1)
                                    qd = qh[:, q5 * 512:(q5 + 1) * 512]
                                    cqv = cq[:, q5 * 512:(q5 + 1) * 512]
                                    sqv = cq[:, HALF + q5 * 512:HALF + (q5 + 1) * 512]
                                    v.tensor_mul(rr[:, 512:1024], ps[q5][0:64, :], cqv)
                                    v.tensor_mul(rr[:, 1024:1536], rr[0:64, 0:512], sqv)
                                    cx.dve.inc(v.tensor_add(qd[0:64, :], rr[:, 512:1024],
                                                            rr[:, 1024:1536]))
                with nc.Block() as blk:
                    peh0, ach0, axh0, dvh0 = cx.pe.n, cx.act.n, cx.aux.n, cx.dve.n
                    def m_sc(q5, i):
                        return peh0 + 33 * q5 + i + 1
                    def m_bc(q5):
                        return peh0 + 33 * q5 + 33
                    def m_exp(q5, i):
                        return ach0 + 33 * q5 + i + 1
                    def m_rdbc(q5):
                        return ach0 + 33 * q5 + 33
                    def m_den(q5, j):
                        return axh0 + 32 * q5 + j + 1
                    def m_recip(q5):
                        return dvh0 + 3 * q5 + 1
                    def m_mults(q5):
                        return dvh0 + 3 * q5 + 3
                    @blk.tensor
                    def _(te):
                        for q5 in range(4):
                            for i in range(33):
                                if i < 32:
                                    if 33 * q5 + i >= 2:
                                        pi = 33 * q5 + i - 2
                                        pq5, pii = divmod(pi if i >= 2 else 33 * (q5 - 1) + 31 + i, 33)
                                        if i >= 2:
                                            te.wait_ge(cx.act.h, m_exp(q5, i - 2))
                                        else:
                                            te.wait_ge(cx.act.h, m_exp(q5 - 1, 30 + i))
                                    b = ps[i % 2]
                                    te.matmul(b[:],
                                              kfm[:, (2 * kv) * S_FULL + i * 128:
                                                  (2 * kv) * S_FULL + (i + 1) * 128],
                                              qh[:, q5 * 512:(q5 + 1) * 512],
                                              start=True, stop=False)
                                    cx.pe.inc(te.matmul(
                                        b[:],
                                        kfm[:, (2 * kv + 1) * S_FULL + i * 128:
                                            (2 * kv + 1) * S_FULL + (i + 1) * 128],
                                        qh[:, HALF + q5 * 512:HALF + (q5 + 1) * 512],
                                        start=False, stop=True))
                                if i >= 1:
                                    j = i - 1
                                    te.wait_ge(cx.act.h, m_exp(q5, j))
                                    if j == 0 and q5 > 0:
                                        te.wait_ge(cx.dve.h, m_mults(q5 - 1))
                                    e = expt[:, (j % 4) * 512:(j % 4 + 1) * 512]
                                    te.matmul(ps[2][:],
                                              vtm[:, j * 512 + kv * 256:
                                                  j * 512 + kv * 256 + 128],
                                              e, start=(j == 0), stop=(j == 31))
                                    te.matmul(ps[3][:],
                                              vtm[:, j * 512 + kv * 256 + 128:
                                                  j * 512 + kv * 256 + 256],
                                              e, start=(j == 0), stop=(j == 31))
                                    cx.aux.inc(te.matmul(
                                        ps[4][0:1, :], P["ones_col"][:],
                                        e, start=(j == 0), stop=(j == 31)))
                            te.wait_ge(cx.dve.h, m_recip(q5))
                            if q5 > 0:
                                te.wait_ge(cx.act.h, m_rdbc(q5 - 1))
                            cx.pe.inc(te.matmul(ps[5][:], P["ones_row"][:], rden[:],
                                                start=True, stop=True))
                    @blk.scalar
                    def _(s):
                        for q5 in range(4):
                            for i in range(32):
                                s.wait_ge(cx.pe.h, m_sc(q5, i))
                                gi = 32 * q5 + i
                                if gi >= 4:
                                    pq5, pj = divmod(gi - 4, 32)
                                    s.wait_ge(cx.aux.h, m_den(pq5, pj))
                                cx.act.inc(s.activation(
                                    expt[:, (i % 4) * 512:(i % 4 + 1) * 512],
                                    ps[i % 2][:], AF.Exp))
                            s.wait_ge(cx.pe.h, m_bc(q5))
                            cx.act.inc(s.activation(rdbc[:], ps[5][:], AF.Copy))
                    @blk.vector
                    def _(v):
                        for q5 in range(4):
                            v.wait_ge(cx.aux.h, m_den(q5, 31))
                            cx.dve.inc(v.reciprocal(rden[:], ps[4][0:1, 0:512]))
                            v.wait_ge(cx.act.h, m_rdbc(q5))
                            cx.dve.inc(v.tensor_mul(aosl(2 * h, q5), ps[2][:], rdbc[:]))
                            cx.dve.inc(v.tensor_mul(aosl(2 * h + 1, q5), ps[3][:], rdbc[:]))

    x1 = cx.sb([128, NT * DM], F32)
    P["x1"] = x1
    wg_bf_d = nc.dram_tensor("wg_bf_d", [DM, 2048], BF16, kind="Internal").ap()
    wo_bf_d = nc.dram_tensor("wo_bf_d", [2048, DM], BF16, kind="Internal").ap()
    with ExitStack() as ph:
        gw2 = cx.sb([128, 2 * 2048], BF16, ph)
        sg = cx.sb([128, 2 * 512], F32, ph)
        aog = cx.sb([128, 16 * 512], BF16, ph)
        wos = cx.sb([128, 2 * 16 * 128], BF16, ph)
        ysb = cx.sb([128, 2 * 512], F32, ph)

        with nc.Block() as blk:
            @blk.gpsimd
            def _(g):
                for c in range(DC):
                    cx.gd.inc(g.dma_start(
                        wg_bf_d[c * 128:(c + 1) * 128, :],
                        t["wq0"][c * 128:(c + 1) * 128, 2048:4096]), 16)
                P["m_wg0"] = cx.gd.n
                for c in range(16):
                    cx.gd.inc(g.dma_start(
                        wo_bf_d[c * 128:(c + 1) * 128, :],
                        t["wo0"][c * 128:(c + 1) * 128, :]), 16)
                P["m_wo0"] = cx.gd.n

        gw_mk = {}
        def gw_stream(sy, t5_, gbh):
            for c in range(DC):
                cx.dm.inc(sy.dma_start(
                    gw2[:, (gbh % 2) * 2048 + c * 256:
                       (gbh % 2) * 2048 + (c + 1) * 256],
                    wg_bf_d[c * 128:(c + 1) * 128,
                            gbh * 256:(gbh + 1) * 256]), 16)
            gw_mk[(t5_, gbh)] = cx.dm.n

        with nc.Block() as blk:
            @blk.sync
            def _(sy):
                sy.wait_ge(cx.gd.h, P["m_wg0"])
                gw_stream(sy, 0, 0)
                gw_stream(sy, 0, 1)

        def wo_stream(sy, oc):
            for fc in range(16):
                cx.dm.inc(sy.dma_start(
                    wos[:, (oc % 2) * 2048 + fc * 128:
                       (oc % 2) * 2048 + (fc + 1) * 128],
                    wo_bf_d[fc * 128:(fc + 1) * 128,
                            oc * 128:(oc + 1) * 128]), 16)
            wo_mk[oc] = cx.dm.n

        for t5 in range(4):
            wo_mk = {}
            with nc.Block() as blk:
                @blk.sync
                def _(sy):
                    for i in range(4):
                        tt = t5 * 4 + i
                        cx.dm.inc(sy.dma_start(
                            x1[:, tt * DM:(tt + 1) * DM],
                            t["x0"][tt * 128:(tt + 1) * 128, :]), 16)
                    sy.wait_ge(cx.dm.h, cx.dm.n)
            with nc.Block() as blk:
                dm0, pe0, ac0, dv0 = cx.dm.n, cx.pe.n, cx.act.n, cx.dve.n
                @blk.sync
                def _(sy):
                    if t5 == 0:
                        sy.wait_ge(cx.gd.h, P["m_wo0"])
                    wo_stream(sy, 0)
                    wo_stream(sy, 1)
                    for gbh in range(2, 8):
                        sy.wait_ge(cx.pe.h, pe0 + 2 * (gbh - 1))
                        gw_stream(sy, t5, gbh)
                @blk.tensor
                def _(te):
                    for gbh in range(8):
                        te.wait_ge(cx.dm.h, gw_mk[(t5, gbh)])
                        for k in range(2):
                            g_ = gbh * 2 + k
                            if g_ >= 2:
                                te.wait_ge(cx.act.h, ac0 + (g_ - 2) + 1)
                            mm = None
                            for c in range(DC):
                                mm = te.matmul(
                                    ps[g_ % 2][:],
                                    gw2[:, (gbh % 2) * 2048 + c * 256 + k * 128:
                                       (gbh % 2) * 2048 + c * 256 + (k + 1) * 128],
                                    P["xn"][:, c * HALF + t5 * 512:
                                            c * HALF + (t5 + 1) * 512],
                                    start=(c == 0), stop=(c == DC - 1))
                            cx.pe.inc(mm)
                @blk.scalar
                def _(s):
                    for g_ in range(16):
                        s.wait_ge(cx.pe.h, pe0 + g_ + 1)
                        if g_ >= 2:
                            s.wait_ge(cx.dve.h, dv0 + (g_ - 2) + 1)
                        cx.act.inc(s.activation(sg[:, (g_ % 2) * 512:(g_ % 2 + 1) * 512],
                                                ps[g_ % 2][:], AF.Silu))
                @blk.vector
                def _(v):
                    for g_ in range(16):
                        v.wait_ge(cx.act.h, ac0 + g_ + 1)
                        cx.dve.inc(v.tensor_mul(aog[:, g_ * 512:(g_ + 1) * 512],
                                                sg[:, (g_ % 2) * 512:(g_ % 2 + 1) * 512],
                                                aosl(g_, t5)))
            with nc.Block() as blk:
                dm0, pe0, ac0, dv0, ax0 = (cx.dm.n, cx.pe.n, cx.act.n,
                                           cx.dve.n, cx.aux.n)
                @blk.sync
                def _(sy):
                    for oc in range(2, DC):
                        sy.wait_ge(cx.pe.h, pe0 + (oc - 1))
                        wo_stream(sy, oc)
                    if t5 < 3:
                        gw_stream(sy, t5 + 1, 0)
                        gw_stream(sy, t5 + 1, 1)
                def emit_tr(te, oc):
                    te.wait_ge(cx.act.h, ac0 + oc + 1)
                    if oc >= 2:
                        te.wait_ge(cx.dve.h, dv0 + 4 * (oc - 2) + 4)
                    for j in range(4):
                        cx.aux.inc(te.transpose(
                            ps[4 + oc % 2][:, j * 128:(j + 1) * 128],
                            ysb[:, (oc % 2) * 512 + j * 128:
                               (oc % 2) * 512 + (j + 1) * 128],
                            P["idf"][:]))
                @blk.tensor
                def _(te):
                    for oc in range(DC):
                        te.wait_ge(cx.dm.h, wo_mk[oc])
                        if oc >= 2:
                            te.wait_ge(cx.act.h, ac0 + (oc - 2) + 1)
                        mm = None
                        for fc in range(16):
                            mm = te.matmul(
                                ps[oc % 2][:],
                                wos[:, (oc % 2) * 2048 + fc * 128:
                                   (oc % 2) * 2048 + (fc + 1) * 128],
                                aog[:, fc * 512:(fc + 1) * 512],
                                start=(fc == 0), stop=(fc == 15))
                        cx.pe.inc(mm)
                        if oc >= 2:
                            emit_tr(te, oc - 2)
                    emit_tr(te, 6)
                    emit_tr(te, 7)
                @blk.scalar
                def _(s):
                    for oc in range(DC):
                        s.wait_ge(cx.pe.h, pe0 + oc + 1)
                        if oc >= 2:
                            s.wait_ge(cx.aux.h, ax0 + 4 * (oc - 2) + 4)
                        cx.act.inc(s.activation(ysb[:, (oc % 2) * 512:(oc % 2 + 1) * 512],
                                                ps[oc % 2][:], AF.Copy))
                @blk.vector
                def _(v):
                    for oc in range(DC):
                        v.wait_ge(cx.aux.h, ax0 + 4 * oc + 4)
                        for j in range(4):
                            tt = t5 * 4 + j
                            xsl = x1[:, tt * DM + oc * 128:tt * DM + (oc + 1) * 128]
                            cx.dve.inc(v.tensor_add(
                                xsl, xsl, ps[4 + oc % 2][:, j * 128:(j + 1) * 128]))

    if DBG:
        with nc.Block() as blk:
            @blk.gpsimd
            def _(g):
                cx.gd.inc(g.dma_start(P["dbg_ao"], ao[:]), 16)
                cx.gd.inc(g.dma_start(P["dbg_x1a"], x1[:]), 16)
                g.wait_ge(cx.gd.h, cx.gd.n)
    aoes.close()
    emit_l0_ffn(cx, t, P, ps, pst)


def emit_l0_ffn(cx, t, P, ps, pst):
    nc = cx.nc
    x1 = P["x1"]
    ph = ExitStack()
    if True:
        wr = cx.sb([128, DC * 8], BF16, ph)
        rt = cx.sb([128, NT * 8], F32, ph)
        ms1 = cx.sb([128, NT * 8], F32, ph)
        ms2 = cx.sb([128, NT * 8], F32, ph)
        a1 = cx.sb([128, NT * 8], F32, ph)
        a2 = cx.sb([128, NT * 8], F32, ph)
        oh1 = cx.sb([128, NT * 8], F32, ph)
        oh2 = cx.sb([128, NT * 8], F32, ph)
        sc8 = cx.sb([128, NT * 8], F32, ph)

        with nc.Block() as blk:
            @blk.gpsimd
            def _(g):
                for c in range(DC):
                    cx.gd.inc(g.dma_start(wr[:, c * 8:(c + 1) * 8],
                                          t["wr0"][c * 128:(c + 1) * 128, :]), 16)
                g.wait_ge(cx.gd.h, cx.gd.n)
        with nc.Block() as blk:
            @blk.tensor
            def _(te):
                for tt in range(NT):
                    for c in range(DC):
                        te.matmul(ps[0][:, tt * 8:(tt + 1) * 8],
                                  P["xn"][:, c * HALF + tt * 128:c * HALF + (tt + 1) * 128],
                                  wr[:, c * 8:(c + 1) * 8],
                                  start=(c == 0), stop=(c == DC - 1))
        with nc.Block() as blk:
            @blk.vector
            def _(v):
                for tt in range(NT):
                    sl = slice(tt * 8, (tt + 1) * 8)
                    v.tensor_copy(rt[:, sl], ps[0][:, sl])
                    v.memset(ms1[:, sl], -1e30)
                    v.memset(ms2[:, sl], -1e30)
                    m = cx.dve.inc(v.reduce_max(ms1[:, tt * 8:tt * 8 + 1],
                                                rt[:, sl], axis=AX))
                    v.wait_ge(cx.dve.h, m)
                    v.match_replace(out=a1[:, sl], in_to_replace=ms1[:, sl],
                                    in_values=rt[:, sl], imm_value=-1e30)
        with nc.Block() as blk:
            @blk.vector
            def _(v):
                for tt in range(NT):
                    sl = slice(tt * 8, (tt + 1) * 8)
                    m = cx.dve.inc(v.tensor_sub(oh1[:, sl], rt[:, sl], a1[:, sl]))
                    v.wait_ge(cx.dve.h, m)
                    v.tensor_scalar(oh1[:, sl], oh1[:, sl], 1e-29, 1.0,
                                    op0=ALU.mult, op1=ALU.min)
                    m = cx.dve.inc(v.reduce_max(ms2[:, tt * 8:tt * 8 + 1],
                                                a1[:, sl], axis=AX))
                    v.wait_ge(cx.dve.h, m)
                    v.match_replace(out=a2[:, sl], in_to_replace=ms2[:, sl],
                                    in_values=a1[:, sl], imm_value=-1e30)
        with nc.Block() as blk:
            dv0 = cx.dve.n
            @blk.vector
            def _(v):
                for tt in range(NT):
                    sl = slice(tt * 8, (tt + 1) * 8)
                    m = cx.dve.inc(v.tensor_sub(oh2[:, sl], a1[:, sl], a2[:, sl]))
                    v.wait_ge(cx.dve.h, m)
                    v.tensor_scalar(oh2[:, sl], oh2[:, sl], 1e-29, 1.0,
                                    op0=ALU.mult, op1=ALU.min)
                    cx.dve.inc(v.tensor_sub(sc8[:, tt * 8:tt * 8 + 1],
                                            ms2[:, tt * 8:tt * 8 + 1],
                                            ms1[:, tt * 8:tt * 8 + 1]))
            @blk.scalar
            def _(s):
                for tt in range(NT):
                    s.wait_ge(cx.dve.h, dv0 + 2 * (tt + 1))
                    s.activation(sc8[:, tt * 8 + 1:tt * 8 + 2],
                                 sc8[:, tt * 8:tt * 8 + 1], AF.Exp)
        with nc.Block() as blk:
            @blk.vector
            def _(v):
                for tt in range(NT):
                    c0 = tt * 8
                    m = cx.dve.inc(v.tensor_scalar_add(sc8[:, c0 + 2:c0 + 3],
                                                       sc8[:, c0 + 1:c0 + 2], 1.0))
                    v.wait_ge(cx.dve.h, m)
                    m = cx.dve.inc(v.reciprocal(sc8[:, c0 + 3:c0 + 4],
                                                sc8[:, c0 + 2:c0 + 3]))
                    v.wait_ge(cx.dve.h, m)
                    v.tensor_mul(sc8[:, c0 + 4:c0 + 5], sc8[:, c0 + 1:c0 + 2],
                                 sc8[:, c0 + 3:c0 + 4])
                    m = cx.dve.inc(v.tensor_scalar(
                        P["wl0"][:, tt * 8:(tt + 1) * 8],
                        oh1[:, tt * 8:(tt + 1) * 8],
                        sc8[:, c0 + 3:c0 + 4], None, op0=ALU.mult))
                    v.wait_ge(cx.dve.h, m)
                    v.scalar_tensor_tensor(out=P["wl0"][:, tt * 8:(tt + 1) * 8],
                                           in0=oh2[:, tt * 8:(tt + 1) * 8],
                                           scalar=sc8[:, c0 + 4:c0 + 5],
                                           in1=P["wl0"][:, tt * 8:(tt + 1) * 8],
                                           op0=ALU.mult, op1=ALU.add)
        if DBG:
            with nc.Block() as blk:
                @blk.gpsimd
                def _(g):
                    cx.gd.inc(g.dma_start(P["dbg_wl0"], P["wl0"][:]), 16)
                    g.wait_ge(cx.gd.h, cx.gd.n)
        ph.close()
        with ExitStack() as pes:
            wd = cx.sb([128, 16 * DM], BF16, pes)
            wgu = cx.sb([128, 4 * 1024], BF16, pes)
            sgb = cx.sb([128, 2 * 512], F32, pes)
            sgu = cx.sb([128, 16 * HALF], BF16, pes)
            for e in range(E0):
                with nc.Block() as blk:
                    gd0, pe0, ac0, dv0 = cx.gd.n, cx.pe.n, cx.act.n, cx.dve.n
                    @blk.gpsimd
                    def _(g):
                        for ic in range(16):
                            bb = ic % 2
                            if ic >= 2:
                                g.wait_ge(cx.pe.h, pe0 + (ic - 1) * 8)
                            for c in range(DC):
                                cx.gd.inc(g.dma_start(
                                    wgu[:, bb * 2048 + c * 128:bb * 2048 + (c + 1) * 128],
                                    t["wg0"][e, c * 128:(c + 1) * 128,
                                             ic * 128:(ic + 1) * 128]), 16)
                                cx.gd.inc(g.dma_start(
                                    wgu[:, bb * 2048 + 1024 + c * 128:
                                       bb * 2048 + 1024 + (c + 1) * 128],
                                    t["wu0"][e, c * 128:(c + 1) * 128,
                                             ic * 128:(ic + 1) * 128]), 16)
                        for ic in range(16):
                            cx.gd.inc(g.dma_start(
                                wd[:, ic * DM:(ic + 1) * DM],
                                t["wd0"][e, ic * 128:(ic + 1) * 128, :]), 16)
                    @blk.tensor
                    def _(te):
                        for ic in range(16):
                            bb = ic % 2
                            te.wait_ge(cx.gd.h, gd0 + (ic + 1) * DC * 32)
                            for t5 in range(4):
                                g_ = ic * 4 + t5
                                if g_ >= 2:
                                    te.wait_ge(cx.dve.h, dv0 + (g_ - 2) + 1)
                                mm = None
                                for c in range(DC):
                                    mm = te.matmul(
                                        ps[g_ % 2][:],
                                        wgu[:, bb * 2048 + c * 128:bb * 2048 + (c + 1) * 128],
                                        P["xn"][:, c * HALF + t5 * 512:c * HALF + (t5 + 1) * 512],
                                        start=(c == 0), stop=(c == DC - 1))
                                cx.pe.inc(mm)
                                mm = None
                                for c in range(DC):
                                    mm = te.matmul(
                                        ps[2 + g_ % 2][:],
                                        wgu[:, bb * 2048 + 1024 + c * 128:
                                           bb * 2048 + 1024 + (c + 1) * 128],
                                        P["xn"][:, c * HALF + t5 * 512:c * HALF + (t5 + 1) * 512],
                                        start=(c == 0), stop=(c == DC - 1))
                                cx.pe.inc(mm)
                    @blk.scalar
                    def _(s):
                        for g_ in range(64):
                            s.wait_ge(cx.pe.h, pe0 + 2 * g_ + 1)
                            if g_ >= 2:
                                s.wait_ge(cx.dve.h, dv0 + (g_ - 2) + 1)
                            cx.act.inc(s.activation(sgb[:, (g_ % 2) * 512:(g_ % 2 + 1) * 512],
                                                    ps[g_ % 2][:], AF.Silu))
                    @blk.vector
                    def _(v):
                        for g_ in range(64):
                            ic, t5 = divmod(g_, 4)
                            v.wait_ge(cx.act.h, ac0 + g_ + 1)
                            v.wait_ge(cx.pe.h, pe0 + 2 * g_ + 2)
                            cx.dve.inc(v.tensor_mul(
                                sgu[:, ic * HALF + t5 * 512:ic * HALF + (t5 + 1) * 512],
                                sgb[:, (g_ % 2) * 512:(g_ % 2 + 1) * 512],
                                ps[2 + g_ % 2][:]))
                gdw = cx.gd.n
                with nc.Block() as blk:
                    pe0, dv0 = cx.pe.n, cx.dve.n
                    @blk.tensor
                    def _(te):
                        te.wait_ge(cx.gd.h, gdw)
                        gidx = 0
                        for tt in range(NT):
                            for hf in range(2):
                                if gidx >= 2:
                                    te.wait_ge(cx.dve.h, dv0 + (gidx - 2) + 1)
                                mm = None
                                for ic in range(16):
                                    mm = te.matmul(
                                        ps[4 + gidx % 2][:],
                                        sgu[:, ic * HALF + tt * 128:ic * HALF + (tt + 1) * 128],
                                        wd[:, ic * DM + hf * 512:ic * DM + (hf + 1) * 512],
                                        start=(ic == 0), stop=(ic == 15))
                                cx.pe.inc(mm)
                                gidx += 1
                    @blk.vector
                    def _(v):
                        gidx = 0
                        for tt in range(NT):
                            for hf in range(2):
                                v.wait_ge(cx.pe.h, pe0 + gidx + 1)
                                xsl = x1[:, tt * DM + hf * 512:tt * DM + (hf + 1) * 512]
                                cx.dve.inc(v.scalar_tensor_tensor(
                                    out=xsl, in0=ps[4 + gidx % 2][:],
                                    scalar=P["wl0"][:, tt * 8 + e:tt * 8 + e + 1],
                                    in1=xsl, op0=ALU.mult, op1=ALU.add))
                                gidx += 1



def host_constants(core):
    par = core % 2
    pos = np.concatenate([np.arange(par * HALF, (par + 1) * HALF),
                          np.arange((1 - par) * HALF, (2 - par) * HALF)]).astype(np.float32)
    inv = np.power(np.float32(10000.0),
                   -(np.arange(0, RD, 2, dtype=np.float32) / np.float32(RD)))
    freqs = pos[:, None] * inv[None, :]
    emb = np.concatenate([freqs, freqs], axis=1)
    cosk = np.ascontiguousarray(np.cos(emb).T)
    sink = np.ascontiguousarray(np.sin(emb).T)
    return {
        "cosk": cosk, "sink": sink,
        "cosq": np.ascontiguousarray(cosk[:, :HALF] * np.float32(0.0625)),
        "sinq": np.ascontiguousarray(sink[:, :HALF] * np.float32(0.0625)),
    }


def declare_l0_inputs(nc):
    t = {}
    t["x0"] = nc.dram_tensor("x0", [S_FULL, DM], F32, kind="ExternalInput").ap()
    t["wq0"] = nc.dram_tensor("wq0", [DM, 4096], F32, kind="ExternalInput").ap()
    t["wk0"] = nc.dram_tensor("wk0", [DM, 512], F32, kind="ExternalInput").ap()
    t["wv0"] = nc.dram_tensor("wv0", [DM, 512], F32, kind="ExternalInput").ap()
    t["wo0"] = nc.dram_tensor("wo0", [2048, DM], F32, kind="ExternalInput").ap()
    t["wr0"] = nc.dram_tensor("wr0", [DM, E0], F32, kind="ExternalInput").ap()
    t["wg0"] = nc.dram_tensor("wg0", [E0, DM, I0], F32, kind="ExternalInput").ap()
    t["wu0"] = nc.dram_tensor("wu0", [E0, DM, I0], F32, kind="ExternalInput").ap()
    t["wd0"] = nc.dram_tensor("wd0", [E0, I0, DM], F32, kind="ExternalInput").ap()
    for n in ("cosk", "sink"):
        t[n] = nc.dram_tensor(n, [RD, S_FULL], F32, kind="ExternalInput").ap()
    for n in ("cosq", "sinq"):
        t[n] = nc.dram_tensor(n, [RD, HALF], F32, kind="ExternalInput").ap()
    return t


def build_l0_program():
    nc = bass.Bass("TRN2", target_bir_lowering=False, debug=False)
    es = ExitStack()
    cx = Cx(nc, es)
    t = declare_l0_inputs(nc)
    y = nc.dram_tensor("x1out", [HALF, DM], F32, kind="ExternalOutput").ap()
    P = {}
    if DBG:
        P["dbg_xn"] = nc.dram_tensor("dbg_xn", [128, DC * HALF], F32, kind="ExternalOutput").ap()
        P["dbg_kfm"] = nc.dram_tensor("dbg_kfm", [128, 4 * S_FULL], F32, kind="ExternalOutput").ap()
        P["dbg_vtm"] = nc.dram_tensor("dbg_vtm", [128, 32 * 512], F32, kind="ExternalOutput").ap()
        P["dbg_ao"] = nc.dram_tensor("dbg_ao", [128, 16 * HALF], F32, kind="ExternalOutput").ap()
        P["dbg_x1a"] = nc.dram_tensor("dbg_x1a", [128, NT * DM], F32, kind="ExternalOutput").ap()
        P["dbg_wl0"] = nc.dram_tensor("dbg_wl0", [128, NT * E0], F32, kind="ExternalOutput").ap()
    P["idf"] = cx.sb([128, 128], F32)
    P["idb"] = cx.sb([128, 128], BF16)
    P["ones_row"] = cx.sb([1, 128], F32)
    P["ones_col"] = cx.sb([128, 1], BF16)
    P["xn"] = cx.sb([128, DC * HALF], BF16)
    P["wl0"] = cx.sb([128, NT * E0], F32)
    ps = [es.enter_context(nc.psum_tensor(f"ps{i}", [128, 512], F32)) for i in range(6)]
    pst = [es.enter_context(nc.psum_tensor("pstA", [128, 1024], BF16)),
           es.enter_context(nc.psum_tensor("pstB", [128, 1024], BF16))]
    emit_identity_ones(nc, P, cx)
    emit_l0(cx, t, P, ps, pst)
    with nc.Block() as blk:
        @blk.sync
        def _(sy):
            for tt in range(NT):
                cx.dm.inc(sy.dma_start(y[tt * 128:(tt + 1) * 128, :],
                                       P["x1"][:, tt * DM:(tt + 1) * DM]), 16)
            sy.wait_ge(cx.dm.h, cx.dm.n)
    es.close()
    return nc


def l0_in_map(core, inputs):
    b, par = core // 2, core % 2
    x = np.asarray(inputs["L0_attn_hidden_states"][b])
    xr = np.concatenate([x[par * HALF:(par + 1) * HALF],
                         x[(1 - par) * HALF:(2 - par) * HALF]], axis=0)
    d = {"x0": np.ascontiguousarray(xr),
         "wq0": inputs["L0_attn_Wq"], "wk0": inputs["L0_attn_Wk"],
         "wv0": inputs["L0_attn_Wv"], "wo0": inputs["L0_attn_Wo"],
         "wr0": inputs["L0_ffn_Wr"], "wg0": inputs["L0_ffn_Wg"],
         "wu0": inputs["L0_ffn_Wu"], "wd0": inputs["L0_ffn_Wd"]}
    d.update(host_constants(core))
    return {k: np.ascontiguousarray(np.asarray(v, dtype=np.float32)) for k, v in d.items()}


def kernel(**inputs):
    return kernel_fused(**inputs)


def kernel_unfused(**inputs):
    nc0 = build_l0_program()
    in_maps = [l0_in_map(c, inputs) for c in range(NC)]
    res = run_bass_kernel_spmd(nc0, in_maps, core_ids=list(range(NC)))
    x1 = np.empty((NB, S_FULL, DM), np.float32)
    for c in range(NC):
        b, par = c // 2, c % 2
        x1[b, par * HALF:(par + 1) * HALF] = res.results[c]["x1out"]
    nc1 = build_l1_program()
    in_maps = [l1_in_map(c, inputs, x1) for c in range(NC)]
    res = run_bass_kernel_spmd(nc1, in_maps, core_ids=list(range(NC)))
    out = np.empty((NB, S_FULL, DM), np.float32)
    for c in range(NC):
        b, par = c // 2, c % 2
        out[b, par * HALF:(par + 1) * HALF] = res.results[c]["yout"]
    return out



def emit_l1(cx, t, P, ps, pst, fused=False):
    nc = cx.nc
    xn1 = P["xn"]
    va_d = nc.dram_tensor("va_d", [NT * 128, 1040], BF16, kind="Internal").ap()
    wqbf_d = nc.dram_tensor("wqbf_d", [DM, DM], BF16, kind="Internal").ap()
    wkbf_d = nc.dram_tensor("wkbf_d", [DM, DM], BF16, kind="Internal").ap()
    wvbf_d = nc.dram_tensor("wvbf_d", [DM, DM], BF16, kind="Internal").ap()
    wobf_d = nc.dram_tensor("wobf_d", [DM, DM], BF16, kind="Internal").ap()
    ewbf_d = nc.dram_tensor("ewbf_d", [E1 * DM, DM], BF16, kind="Internal").ap()
    P["ewbf_d"] = ewbf_d

    bqb = cx.sb([128, DM], F32)
    bo_fm = cx.sb([128, 8], F32)
    gbb = cx.sb([128, 8], F32)
    gwT = cx.sb([128, DC * 8], BF16)
    masku = cx.sb([128, 128], BF16)
    S_m = cx.sb([128, 8 * 65], F32)
    S_bf = cx.sb([128, 8 * 65], BF16)
    Srep = cx.sb([128, 8 * 65], F32)
    wl1 = cx.sb([128, NT * 8], F32)
    P["gwT"], P["gbb"], P["wl1"] = gwT, gbb, wl1
    ktm = cx.sb([128, NT * 1024], BF16)
    is2 = cx.sb([128, 1], F32)

    c23 = ExitStack()
    bkb = cx.sb([128, DM], F32, c23)
    bvb = cx.sb([128, DM], F32, c23)
    with ExitStack() as ph:
        brow = cx.sb([1, 3 * DM + 8], F32, ph)
        bo8 = cx.sb([8, 128], F32, ph)
        gw8 = cx.sb([8, DM], BF16, ph)
        wtmp = cx.sb([128, 2 * DM], BF16, ph)
        wout = cx.sb([128, DC * 128], BF16, ph)

        with nc.Block() as blk:
            @blk.sync
            def _(sy):
                cx.dm.inc(sy.dma_start(brow[0:1, 0:DM], t["bq1"].rearrange("(o a) -> o a", o=1)), 16)
                cx.dm.inc(sy.dma_start(brow[0:1, DM:2 * DM], t["bk1"].rearrange("(o a) -> o a", o=1)), 16)
                cx.dm.inc(sy.dma_start(brow[0:1, 2 * DM:3 * DM], t["bv1"].rearrange("(o a) -> o a", o=1)), 16)
                cx.dm.inc(sy.dma_start(brow[0:1, 3 * DM:3 * DM + 8], t["gb1"].rearrange("(o a) -> o a", o=1)), 16)
                cx.dm.inc(sy.dma_start(bo8[:], t["bo1"].rearrange("(a b) -> a b", b=128)), 16)
                cx.dm.inc(sy.dma_start(is2[:], t["is2nd"]), 16)
                sy.wait_ge(cx.dm.h, cx.dm.n)
            @blk.gpsimd
            def _(g):
                cx.gd.inc(g.dma_start(gw8[:], t["gw1"]), 16)
                cx.gd.inc(g.dma_start(masku[:], t["masku"]), 16)
                g.wait_ge(cx.gd.h, cx.gd.n)
            @blk.vector
            def _(v):
                v.memset(Srep[:], 0.0)
        with nc.Block() as blk:
            pe0 = cx.pe.n
            @blk.tensor
            def _(te):
                for i in range(3):
                    for hf in range(2):
                        cx.pe.inc(te.matmul(
                            ps[2 * i + hf][:], P["ones_row"][:],
                            brow[0:1, i * DM + hf * 512:i * DM + (hf + 1) * 512],
                            start=True, stop=True))
            @blk.scalar
            def _(s):
                dsts = [bqb, bkb, bvb]
                for i in range(3):
                    for hf in range(2):
                        s.wait_ge(cx.pe.h, pe0 + 2 * i + hf + 1)
                        s.activation(dsts[i][:, hf * 512:(hf + 1) * 512],
                                     ps[2 * i + hf][:], AF.Copy)
        with nc.Block() as blk:
            pe0 = cx.pe.n
            @blk.tensor
            def _(te):
                cx.pe.inc(te.matmul(ps[0][:, 0:8], P["ones_row"][:],
                                    brow[0:1, 3 * DM:3 * DM + 8],
                                    start=True, stop=True))
            @blk.scalar
            def _(s):
                s.wait_ge(cx.pe.h, pe0 + 1)
                s.activation(gbb[:], ps[0][:, 0:8], AF.Copy)
        with nc.Block() as blk:
            pe0, dv0 = cx.pe.n, cx.dve.n
            @blk.tensor
            def _(te):
                cx.pe.inc(te.transpose(ps[5][:, 0:8], bo8[:], P["idf"][0:8, 0:8]))
                for c in range(DC):
                    if c >= 2:
                        te.wait_ge(cx.dve.h, dv0 + (c - 2) + 1)
                    cx.pe.inc(te.transpose(pst[c % 2][:, 0:8],
                                           gw8[:, c * 128:(c + 1) * 128],
                                           P["idb"][0:8, 0:8]))
            @blk.vector
            def _(v):
                v.wait_ge(cx.pe.h, pe0 + 1)
                v.tensor_copy(bo_fm[:], ps[5][:, 0:8])
                for c in range(DC):
                    v.wait_ge(cx.pe.h, pe0 + c + 2)
                    cx.dve.inc(v.tensor_copy(gwT[:, c * 8:(c + 1) * 8],
                                             pst[c % 2][:, 0:8]))
        with nc.Block() as blk:
            @blk.gpsimd
            def _(g):
                P["m_wk"] = cx.gd.inc(g.dma_start(wkbf_d, t["wk1"]), 16)
                P["m_wv"] = cx.gd.inc(g.dma_start(wvbf_d, t["wv1"]), 16)
                P["m_wq"] = cx.gd.inc(g.dma_start(wqbf_d, t["wq1"]), 16)
                P["m_wo"] = cx.gd.inc(g.dma_start(wobf_d, t["wo1"]), 16)
                for e in range(E1):
                    cx.gd.inc(g.dma_start(ewbf_d[e * DM:(e + 1) * DM, :],
                                          t["ew1"][e]), 16)
                P["m_ew"] = cx.gd.n

    if fused:
        emit_l1_kv_fused(cx, t, P, ps, pst, wkbf_d, wvbf_d, bkb, bvb,
                         ktm, Srep, va_d)
    if not fused:
     with ExitStack() as ph:
        xtmp = cx.sb([128, DM], F32, ph)
        xnt = cx.sb([128, DM], BF16, ph)
        sq = cx.sb([128, DM], F32, ph)
        stw = cx.sb([128, 8], F32, ph)
        xfm = cx.sb([128, DM], BF16, ph)
        wstr = cx.sb([128, 2 * DM], BF16, ph)
        qf = cx.sb([128, DM], F32, ph)
        rb = cx.sb([128, 512], F32, ph)
        mn = cx.sb([128, 512], F32, ph)
        eb = cx.sb([128, 512], F32, ph)
        kb = cx.sb([128, DM], BF16, ph)
        vb = cx.sb([128, DM], BF16, ph)
        vao = cx.sb([128, 1040], BF16, ph)

        with nc.Block() as blk:
            @blk.vector
            def _(v):
                v.memset(vao[:], 1.0)

        for half in ([0] if fused else [0, 1]):
            for ti in range(NT):
                row0 = half * HALF + ti * 128
                if fused:
                    xsrc = P["x2"][:, ti * DM:(ti + 1) * DM]
                else:
                    with nc.Block() as blk:
                        @blk.sync
                        def _(sy):
                            cx.dm.inc(sy.dma_start(xtmp[:],
                                                   t["x1r"][row0:row0 + 128, :]), 16)
                            sy.wait_ge(cx.dm.h, cx.dm.n)
                    xsrc = xtmp[:]
                emit_ln_blocks(cx, lambda i: xsrc, lambda i: xnt[:], sq, stw, 1)
                if half == 0:
                    jobs = [(xnt[:, c * 128:(c + 1) * 128],
                             xn1[:, c * HALF + ti * 128:c * HALF + (ti + 1) * 128])
                            for c in range(DC)]
                else:
                    jobs = [(xnt[:, c * 128:(c + 1) * 128],
                             xfm[:, c * 128:(c + 1) * 128]) for c in range(DC)]
                emit_transpose_block(cx, P, pst, jobs)
                if half == 0:
                    def lhs(c, ti=ti):
                        return xn1[:, c * HALF + ti * 128:c * HALF + (ti + 1) * 128]
                else:
                    def lhs(c):
                        return xfm[:, c * 128:(c + 1) * 128]
                for wd_d, bias_bc, dst, phi in ((wkbf_d, bkb, kb, True),
                                                (wvbf_d, bvb, vb, False)):
                    with nc.Block() as blk:
                        pe0, dv0, ac0 = cx.pe.n, cx.dve.n, cx.act.n
                        dm0 = cx.dm.n
                        @blk.sync
                        def _(sy):
                            for hc in range(DC):
                                if hc >= 2:
                                    sy.wait_ge(cx.pe.h, pe0 + (hc - 1))
                                cx.dm.inc(sy.dma_start_transpose(
                                    wstr[:, (hc % 2) * DM:(hc % 2 + 1) * DM],
                                    wd_d[:, hc * 128:(hc + 1) * 128]), 16)
                        @blk.tensor
                        def _(te):
                            for hc in range(DC):
                                te.wait_ge(cx.dm.h, dm0 + (hc + 1) * 16)
                                mm = None
                                for hf in range(2):
                                    mm = te.matmul(
                                        ps[hf][:], lhs(hc),
                                        wstr[:, (hc % 2) * DM + hf * 512:
                                             (hc % 2) * DM + (hf + 1) * 512],
                                        start=(hc == 0), stop=(hc == DC - 1))
                                cx.pe.inc(mm)
                        @blk.vector
                        def _(v):
                            v.wait_ge(cx.pe.h, pe0 + DC)
                            cx.dve.inc(v.tensor_add(qf[:, 0:512], ps[0][:],
                                                    bias_bc[:, 0:512]))
                            cx.dve.inc(v.tensor_add(qf[:, 512:1024], ps[1][:],
                                                    bias_bc[:, 512:1024]))
                            if not phi:
                                v.tensor_copy(dst[:], qf[:])
                            else:
                                for hf in range(2):
                                    sl = slice(hf * 512, (hf + 1) * 512)
                                    v.tensor_relu(rb[:], qf[:, sl])
                                    cx.dve.inc(v.tensor_scalar_min(mn[:], qf[:, sl], 0.0))
                                    v.wait_ge(cx.act.h, ac0 + hf + 1)
                                    cx.dve.inc(v.tensor_add(dst[:, sl], eb[:], rb[:]))
                        if phi:
                            @blk.scalar
                            def _(s):
                                for hf in range(2):
                                    s.wait_ge(cx.dve.h, dv0 + 3 + 2 * hf)
                                    cx.act.inc(s.activation(eb[:], mn[:], AF.Exp))
                do_state = (half == 1) or fused
                with nc.Block() as blk:
                    pe0, dv0 = cx.pe.n, cx.dve.n

                    @blk.vector
                    def _(v):
                        if half == 0:
                            v.tensor_copy(ktm[:, ti * 1024:(ti + 1) * 1024], kb[:])
                        for h in range(H1):
                            cx.dve.inc(v.tensor_copy(
                                vao[:, h * 65:h * 65 + 64],
                                vb[:, h * 64:(h + 1) * 64]))
                        if do_state:
                            for h in range(H1):
                                v.wait_ge(cx.pe.h, pe0 + h + 1)
                                base, co = (h % 2) * 64, (h // 2) * 65
                                dstv = Srep[base:base + 64, co:co + 65]
                                cx.dve.inc(v.tensor_add(dstv, dstv,
                                                        ps[4 + h % 2][0:64, 0:65]))
                    if do_state:
                        @blk.tensor
                        def _(te):
                            for h in range(H1):
                                te.wait_ge(cx.dve.h, dv0 + h + 1)
                                if h >= 2:
                                    te.wait_ge(cx.dve.h, dv0 + 16 + (h - 2) + 1)
                                cx.pe.inc(te.matmul(
                                    ps[4 + h % 2][0:64, 0:65],
                                    kb[:, h * 64:(h + 1) * 64],
                                    vao[:, h * 65:(h + 1) * 65],
                                    start=True, stop=True))
                if half == 0:
                    with nc.Block() as blk:
                        @blk.sync
                        def _(sy):
                            cx.dm.inc(sy.dma_start(
                                va_d[ti * 128:(ti + 1) * 128, :], vao[:]), 16)
                            sy.wait_ge(cx.dm.h, cx.dm.n)
    c23.close()

    if "x2" not in P:
        P["x2"] = cx.sb([128, NT * DM], F32)
        with nc.Block() as blk:
            @blk.sync
            def _(sy):
                for tt in range(NT):
                    cx.dm.inc(sy.dma_start(P["x2"][:, tt * DM:(tt + 1) * DM],
                                           t["x1r"][tt * 128:(tt + 1) * 128, :]), 16)
                sy.wait_ge(cx.dm.h, cx.dm.n)
    x2 = P["x2"]

    hses = ExitStack()
    if fused:
        emit_handshake_send(cx, t, P, Srep, hses)

    emit_l1_ffn(cx, t, P, ps, pst)

    if fused:
        emit_handshake_recv(cx, t, P, Srep)
    hses.close()

    with nc.Block() as blk:
        dv0 = cx.dve.n
        @blk.vector
        def _(v):
            cx.dve.inc(v.tensor_scalar(S_m[:], Srep[:], is2[:, 0:1], None,
                                       op0=ALU.mult))
            v.wait_ge(cx.dve.h, dv0 + 1)
            v.tensor_copy(S_bf[:], S_m[:])

    with ExitStack() as ph:
        vstr = cx.sb([128, 2 * 1040], BF16, ph)
        qf = cx.sb([128, DM], F32, ph)
        rb = cx.sb([128, 512], F32, ph)
        mn = cx.sb([128, 512], F32, ph)
        eb = cx.sb([128, 512], F32, ph)
        qb = cx.sb([128, DM], BF16, ph)
        qfm = cx.sb([128, DM], BF16, ph)
        kfj = cx.sb([128, DM], BF16, ph)
        scm = cx.sb([128, 2 * 256], BF16, ph)
        rden = cx.sb([128, 4], F32, ph)
        oub = cx.sb([128, DM], BF16, ph)
        ofm = cx.sb([128, DM], BF16, ph)
        ysb = cx.sb([128, 8 * 128], F32, ph)
        wor = cx.sb([128, DC * DM], BF16, ph)
        wqr = cx.sb([128, DC * DM], BF16, ph)

        v_mk = {}
        def v_load(sy, j_):
            cx.dm.inc(sy.dma_start(
                vstr[:, (j_ % 2) * 1040:(j_ % 2 + 1) * 1040],
                va_d[j_ * 128:(j_ + 1) * 128, :]), 16)
            v_mk[j_] = cx.dm.n

        with nc.Block() as blk:
            @blk.sync
            def _(sy):
                sy.wait_ge(cx.gd.h, P["m_wq"])
                for hc in range(DC):
                    cx.dm.inc(sy.dma_start_transpose(
                        wqr[:, hc * DM:(hc + 1) * DM],
                        wqbf_d[:, hc * 128:(hc + 1) * 128]), 16)
                m_wqrl_ = cx.dm.n
                v_load(sy, 0)
                sy.wait_ge(cx.gd.h, P["m_wo"])
                for dmc in range(DC):
                    cx.dm.inc(sy.dma_start_transpose(
                        wor[:, dmc * DM:(dmc + 1) * DM],
                        wobf_d[:, dmc * 128:(dmc + 1) * 128]), 16)
        m_wqrl = v_mk[0] - 16
        m_worl = cx.dm.n

        for j in range(NT):
            tcol = slice(j * 128, (j + 1) * 128)
            with nc.Block() as blk:
                pe0, dv0, ac0 = cx.pe.n, cx.dve.n, cx.act.n
                dm0 = cx.dm.n
                @blk.sync
                def _(sy):
                    if j + 1 < NT:
                        v_load(sy, j + 1)
                @blk.tensor
                def _(te):
                    if j == 0:
                        te.wait_ge(cx.dm.h, m_wqrl)
                    te.wait_ge(cx.dm.h, v_mk[j])
                    for hc in range(DC):
                        mm = None
                        for hf in range(2):
                            mm = te.matmul(
                                ps[hf][:],
                                xn1[:, hc * HALF + j * 128:hc * HALF + (j + 1) * 128],
                                wqr[:, hc * DM + hf * 512:
                                    hc * DM + (hf + 1) * 512],
                                start=(hc == 0), stop=(hc == DC - 1))
                        cx.pe.inc(mm)
                @blk.vector
                def _(v):
                    v.wait_ge(cx.pe.h, pe0 + DC)
                    cx.dve.inc(v.tensor_add(qf[:, 0:512], ps[0][:], bqb[:, 0:512]))
                    cx.dve.inc(v.tensor_add(qf[:, 512:1024], ps[1][:], bqb[:, 512:1024]))
                    for hf in range(2):
                        sl = slice(hf * 512, (hf + 1) * 512)
                        v.tensor_relu(rb[:], qf[:, sl])
                        cx.dve.inc(v.tensor_scalar_min(mn[:], qf[:, sl], 0.0))
                        v.wait_ge(cx.act.h, ac0 + hf + 1)
                        cx.dve.inc(v.tensor_add(qb[:, sl], eb[:], rb[:]))
                @blk.scalar
                def _(s):
                    for hf in range(2):
                        s.wait_ge(cx.dve.h, dv0 + 3 + 2 * hf)
                        cx.act.inc(s.activation(eb[:], mn[:], AF.Exp))
            jobs = []
            for p in range(8):
                jobs.append((qb[:, p * 128:(p + 1) * 128], qfm[:, p * 128:(p + 1) * 128]))
                jobs.append((ktm[:, j * 1024 + p * 128:j * 1024 + (p + 1) * 128],
                             kfj[:, p * 128:(p + 1) * 128]))
            emit_transpose_block(cx, P, pst, jobs)
            with nc.Block() as blk:
                pe0, dv0, ac0 = cx.pe.n, cx.dve.n, cx.act.n
                NP = H1 // 2
                mk_sc, mk_num, mk_st = {}, {}, {}
                @blk.tensor
                def _(te):
                    def sc(p):
                        if p >= 1:
                            te.wait_ge(cx.dve.h, dv0 + 6 * (p - 1) + 1)
                        mm = None
                        for hb in range(2):
                            mm = te.matmul(
                                ps[hb][:, 0:128],
                                kfj[hb * 64:(hb + 1) * 64, p * 128:(p + 1) * 128],
                                qfm[hb * 64:(hb + 1) * 64, p * 128:(p + 1) * 128],
                                start=True, stop=True)
                        cx.pe.inc(mm)
                        mk_sc[p] = cx.pe.n
                    def num(p):
                        g, co = p % 2, p * 65
                        te.wait_ge(cx.dve.h, dv0 + 6 * p + 1)
                        if p >= 2:
                            te.wait_ge(cx.act.h, ac0 + 2 * p - 2)
                        mm = None
                        for hb in range(2):
                            h = 2 * p + hb
                            te.matmul(
                                ps[2 + g][:, hb * 128:hb * 128 + 65],
                                scm[:, g * 256 + hb * 128:g * 256 + (hb + 1) * 128],
                                vstr[:, (j % 2) * 1040 + h * 65:(j % 2) * 1040 + (h + 1) * 65],
                                start=True, stop=False)
                            mm = te.matmul(
                                ps[2 + g][:, hb * 128:hb * 128 + 65],
                                qfm[hb * 64:(hb + 1) * 64, p * 128:(p + 1) * 128],
                                S_bf[hb * 64:(hb + 1) * 64, co:co + 65],
                                start=False, stop=True)
                        cx.pe.inc(mm)
                        mk_num[p] = cx.pe.n
                    def st(p):
                        g = p % 2
                        if p >= 2:
                            te.wait_ge(cx.dve.h, dv0 + 6 * (p - 2) + 5)
                        mm = None
                        for hb in range(2):
                            h = 2 * p + hb
                            mm = te.matmul(
                                ps[4 + g][0:64, hb * 128:hb * 128 + 65],
                                ktm[:, j * 1024 + h * 64:j * 1024 + (h + 1) * 64],
                                vstr[:, (j % 2) * 1040 + h * 65:(j % 2) * 1040 + (h + 1) * 65],
                                start=True, stop=True)
                        cx.pe.inc(mm)
                        mk_st[p] = cx.pe.n
                    for p in range(NP):
                        sc(p)
                        num(p)
                        st(p)
                @blk.vector
                def _(v):
                    for p in range(NP):
                        g, co = p % 2, p * 65
                        v.wait_ge(cx.pe.h, mk_sc[p])
                        v.tensor_mul(scm[:, g * 256:g * 256 + 128],
                                     ps[0][:, 0:128], masku[:])
                        cx.dve.inc(v.tensor_mul(scm[:, g * 256 + 128:g * 256 + 256],
                                                ps[1][:, 0:128], masku[:]))
                        v.wait_ge(cx.pe.h, mk_num[p])
                        if p >= 2:
                            v.wait_ge(cx.act.h, ac0 + 2 * p - 2)
                        cx.dve.inc(v.reciprocal(rden[:, 2 * g:2 * g + 1],
                                                ps[2 + g][:, 64:65]))
                        v.wait_ge(cx.dve.h, cx.dve.n)
                        cx.dve.inc(v.reciprocal(rden[:, 2 * g + 1:2 * g + 2],
                                                ps[2 + g][:, 192:193]))
                        v.wait_ge(cx.pe.h, mk_st[p])
                        d0 = S_m[0:64, co:co + 65]
                        cx.dve.inc(v.tensor_add(d0, d0, ps[4 + g][0:64, 0:65]))
                        d1 = S_m[64:128, co:co + 65]
                        cx.dve.inc(v.tensor_add(d1, d1, ps[4 + g][0:64, 128:193]))
                        m = cx.dve.n
                        v.wait_ge(cx.dve.h, m)
                        cx.dve.inc(v.tensor_copy(S_bf[:, co:co + 65],
                                                 S_m[:, co:co + 65]))
                @blk.scalar
                def _(s):
                    for p in range(NP):
                        g = p % 2
                        s.wait_ge(cx.dve.h, dv0 + 6 * p + 3)
                        cx.act.inc(s.activation(
                            oub[:, (2 * p) * 64:(2 * p) * 64 + 64],
                            ps[2 + g][:, 0:64], AF.Copy,
                            scale=rden[:, 2 * g:2 * g + 1]))
                        cx.act.inc(s.activation(
                            oub[:, (2 * p + 1) * 64:(2 * p + 1) * 64 + 64],
                            ps[2 + g][:, 128:192], AF.Copy,
                            scale=rden[:, 2 * g + 1:2 * g + 2]))
            with nc.Block() as blk:
                pe0, dv0, ac0, axp = cx.pe.n, cx.dve.n, cx.act.n, cx.aux.n
                @blk.tensor
                def _(te):
                    for gtr in range(2):
                        for q in range(4):
                            p = gtr * 4 + q
                            cx.aux.inc(te.transpose(
                                pst[gtr % 2][:, q * 128:(q + 1) * 128],
                                oub[:, p * 128:(p + 1) * 128], P["idb"][:]))
                    if j == 0:
                        te.wait_ge(cx.dm.h, m_worl)
                    for dmc in range(DC):
                        te.wait_ge(cx.dve.h, dv0 + 4 * (dmc // 4) + 4)
                        mm = None
                        for oc in range(DC):
                            mm = te.matmul(
                                ps[oc // 4][:, (oc % 4) * 128:(oc % 4 + 1) * 128],
                                wor[:, dmc * DM + oc * 128:
                                    dmc * DM + (oc + 1) * 128],
                                ofm[:, dmc * 128:(dmc + 1) * 128],
                                start=(dmc == 0 and oc % 4 == 0), stop=(dmc == DC - 1))
                        cx.pe.inc(mm)
                    for g in range(2):
                        te.wait_ge(cx.dve.h, dv0 + 8 + 4 * (g + 1))
                        for q in range(4):
                            oc = g * 4 + q
                            cx.aux.inc(te.transpose(
                                ps[2 + g % 2][:, q * 128:(q + 1) * 128],
                                ysb[:, oc * 128:(oc + 1) * 128], P["idf"][:]))
                @blk.vector
                def _(v):
                    for gtr in range(2):
                        v.wait_ge(cx.aux.h, axp + 4 * (gtr + 1))
                        for q in range(4):
                            p = gtr * 4 + q
                            cx.dve.inc(v.tensor_copy(
                                ofm[:, p * 128:(p + 1) * 128],
                                pst[gtr % 2][:, q * 128:(q + 1) * 128]))
                    for oc in range(DC):
                        v.wait_ge(cx.pe.h, pe0 + DC)
                        cx.dve.inc(v.tensor_scalar(
                            ysb[:, oc * 128:(oc + 1) * 128],
                            ps[oc // 4][:, (oc % 4) * 128:(oc % 4 + 1) * 128],
                            bo_fm[:, oc:oc + 1], None, op0=ALU.add))
                    ax0 = cx.aux.n - 8
                    for g in range(2):
                        v.wait_ge(cx.aux.h, ax0 + 4 * (g + 1))
                        for q in range(4):
                            oc = g * 4 + q
                            xsl = x2[:, j * DM + oc * 128:j * DM + (oc + 1) * 128]
                            cx.dve.inc(v.tensor_add(xsl, xsl,
                                                    ps[2 + g % 2][:, q * 128:(q + 1) * 128]))


def emit_l1_kv_fused(cx, t, P, ps, pst, wkbf_d, wvbf_d, bkb, bvb,
                     ktm, Srep, va_d):
    nc = cx.nc
    xn1 = P["xn"]
    x2 = P["x2"]
    with ExitStack() as ph:
        xnt = cx.sb([128, 4 * DM], BF16, ph)
        sq = cx.sb([128, DM], F32, ph)
        stw = cx.sb([128, 4 * 8], F32, ph)
        wres = cx.sb([128, DC * DM], BF16, ph)
        rb = cx.sb([128, 512], F32, ph)
        mn = cx.sb([128, 512], F32, ph)
        eb = cx.sb([128, 512], F32, ph)
        qf = cx.sb([128, DM], F32, ph)
        vao = cx.sb([128, 2 * 1040], BF16, ph)

        for g4 in range(4):
            emit_ln_blocks(cx,
                           lambda i, g4=g4: x2[:, (g4 * 4 + i) * DM:(g4 * 4 + i + 1) * DM],
                           lambda i: xnt[:, i * DM:(i + 1) * DM], sq, stw, 4)
            jobs = []
            for i in range(4):
                ti = g4 * 4 + i
                for c in range(DC):
                    jobs.append((xnt[:, i * DM + c * 128:i * DM + (c + 1) * 128],
                                 xn1[:, c * HALF + ti * 128:c * HALF + (ti + 1) * 128]))
            emit_transpose_block(cx, P, pst, jobs)

        with nc.Block() as blk:
            @blk.sync
            def _(sy):
                sy.wait_ge(cx.gd.h, P["m_wk"])
                for hc in range(DC):
                    cx.dm.inc(sy.dma_start_transpose(
                        wres[:, hc * DM:(hc + 1) * DM],
                        wkbf_d[:, hc * 128:(hc + 1) * 128]), 16)
                sy.wait_ge(cx.dm.h, cx.dm.n)
        with nc.Block() as blk:
            pe0, dv0, ac0 = cx.pe.n, cx.dve.n, cx.act.n
            @blk.tensor
            def _(te):
                for ti in range(NT):
                    if ti >= 3:
                        te.wait_ge(cx.dve.h, dv0 + 6 * (ti - 3) + 2)
                    b0 = (ti % 3) * 2
                    mm = None
                    for hc in range(DC):
                        for hf in range(2):
                            mm = te.matmul(
                                ps[b0 + hf][:],
                                xn1[:, hc * HALF + ti * 128:hc * HALF + (ti + 1) * 128],
                                wres[:, hc * DM + hf * 512:hc * DM + (hf + 1) * 512],
                                start=(hc == 0), stop=(hc == DC - 1))
                    cx.pe.inc(mm)
            @blk.vector
            def _(v):
                for ti in range(NT):
                    b0 = (ti % 3) * 2
                    v.wait_ge(cx.pe.h, pe0 + ti + 1)
                    cx.dve.inc(v.tensor_add(qf[:, 0:512], ps[b0][:], bkb[:, 0:512]))
                    cx.dve.inc(v.tensor_add(qf[:, 512:1024], ps[b0 + 1][:],
                                            bkb[:, 512:1024]))
                    for hf in range(2):
                        sl = slice(hf * 512, (hf + 1) * 512)
                        v.tensor_relu(rb[:], qf[:, sl])
                        cx.dve.inc(v.tensor_scalar_min(mn[:], qf[:, sl], 0.0))
                        v.wait_ge(cx.act.h, ac0 + 2 * ti + hf + 1)
                        cx.dve.inc(v.tensor_add(
                            ktm[:, ti * 1024 + hf * 512:ti * 1024 + (hf + 1) * 512],
                            eb[:], rb[:]))
            @blk.scalar
            def _(s):
                for ti in range(NT):
                    for hf in range(2):
                        s.wait_ge(cx.dve.h, dv0 + 6 * ti + 3 + 2 * hf)
                        cx.act.inc(s.activation(eb[:], mn[:], AF.Exp))

        with nc.Block() as blk:
            @blk.sync
            def _(sy):
                sy.wait_ge(cx.gd.h, P["m_wv"])
                for hc in range(DC):
                    cx.dm.inc(sy.dma_start_transpose(
                        wres[:, hc * DM:(hc + 1) * DM],
                        wvbf_d[:, hc * 128:(hc + 1) * 128]), 16)
                sy.wait_ge(cx.dm.h, cx.dm.n)
            @blk.vector
            def _(v):
                v.memset(vao[:], 1.0)
        def spk(h):
            return (4 + h % 2, ((h // 2) % 2) * 64, (h // 4) * 65)
        with nc.Block() as blk:
            pe0, dv0, dm0 = cx.pe.n, cx.dve.n, cx.dm.n
            vmm_marks = []
            def emit_vmm(te, ti):
                if ti >= 2:
                    te.wait_ge(cx.dve.h, dv0 + 16 * (ti - 2) + 16)
                b0 = (ti % 2) * 2
                mm = None
                for hc in range(DC):
                    for hf in range(2):
                        mm = te.matmul(
                            ps[b0 + hf][:],
                            xn1[:, hc * HALF + ti * 128:hc * HALF + (ti + 1) * 128],
                            wres[:, hc * DM + hf * 512:hc * DM + (hf + 1) * 512],
                            start=(hc == 0), stop=(hc == DC - 1))
                vmm_marks.append(cx.pe.inc(mm))
            def emit_state(te, ti):
                te.wait_ge(cx.dve.h, dv0 + 16 * ti + 16)
                mm = None
                for h in range(H1):
                    bk, rr0, cc0 = spk(h)
                    mm = te.matmul(
                        ps[bk][rr0:rr0 + 64, cc0:cc0 + 65],
                        ktm[:, ti * 1024 + h * 64:ti * 1024 + (h + 1) * 64],
                        vao[:, (ti % 2) * 1040 + h * 65:(ti % 2) * 1040 + (h + 1) * 65],
                        start=(ti == 0 and h < 2), stop=(ti == NT - 1))
                cx.pe.inc(mm)
            @blk.tensor
            def _(te):
                emit_vmm(te, 0)
                for ti in range(1, NT):
                    emit_vmm(te, ti)
                    emit_state(te, ti - 1)
                emit_state(te, NT - 1)
            @blk.vector
            def _(v):
                for ti in range(NT):
                    b0 = (ti % 2) * 2
                    v.wait_ge(cx.pe.h, vmm_marks[ti])
                    if ti >= 2:
                        v.wait_ge(cx.dm.h, dm0 + 16 * (ti - 1))
                    for h in range(H1):
                        hf, off = divmod(h * 64, 512)
                        cx.dve.inc(v.tensor_add(
                            vao[:, (ti % 2) * 1040 + h * 65:(ti % 2) * 1040 + h * 65 + 64],
                            ps[b0 + hf][:, off:off + 64],
                            bvb[:, h * 64:(h + 1) * 64]))
            @blk.sync
            def _(sy):
                for ti in range(NT):
                    sy.wait_ge(cx.dve.h, dv0 + 16 * ti + 16)
                    cx.dm.inc(sy.dma_start(va_d[ti * 128:(ti + 1) * 128, :],
                                           vao[:, (ti % 2) * 1040:(ti % 2 + 1) * 1040]), 16)
        with nc.Block() as blk:
            @blk.vector
            def _(v):
                for h in range(H1):
                    bk, rr0, cc0 = spk(h)
                    base, co = (h % 2) * 64, (h // 2) * 65
                    v.tensor_copy(Srep[base:base + 64, co:co + 65],
                                  ps[bk][rr0:rr0 + 64, cc0:cc0 + 65])


def emit_l1_ffn(cx, t, P, ps, pst):
    nc = cx.nc
    x2 = P["x2"]
    xn1 = P["xn"]
    gwT, gbb, wl1 = P["gwT"], P["gbb"], P["wl1"]
    with ExitStack() as ph:
        ebs = cx.sb([8, DM], F32, ph)
        gt = cx.sb([128, NT * 8], F32, ph)
        ms1 = cx.sb([128, NT * 8], F32, ph)
        ms2 = cx.sb([128, NT * 8], F32, ph)
        a1 = cx.sb([128, NT * 8], F32, ph)
        a2 = cx.sb([128, NT * 8], F32, ph)
        oh1 = cx.sb([128, NT * 8], F32, ph)
        sc8 = cx.sb([128, NT * 8], F32, ph)
        wT8 = cx.sb([8, NT * 128], F32, ph)
        ewT = cx.sb([128, 2 * DC * DM], BF16, ph)
        wtmp = cx.sb([128, 2 * DM], BF16, ph)

        with nc.Block() as blk:
            @blk.sync
            def _(sy):
                cx.dm.inc(sy.dma_start(ebs[:], t["eb1"]), 16)
                sy.wait_ge(cx.dm.h, cx.dm.n)
        with nc.Block() as blk:
            @blk.tensor
            def _(te):
                for tt in range(NT):
                    for c in range(DC):
                        te.matmul(ps[0][:, tt * 8:(tt + 1) * 8],
                                  xn1[:, c * HALF + tt * 128:c * HALF + (tt + 1) * 128],
                                  gwT[:, c * 8:(c + 1) * 8],
                                  start=(c == 0 and tt == 0), stop=(c == DC - 1))
        with nc.Block() as blk:
            dv0 = cx.dve.n
            @blk.vector
            def _(v):
                for tt in range(NT):
                    sl = slice(tt * 8, (tt + 1) * 8)
                    m = cx.dve.inc(v.tensor_add(gt[:, sl], ps[0][:, sl], gbb[:]))
                    v.wait_ge(cx.dve.h, m)
                    m = cx.dve.inc(v.reduce_max(ms1[:, tt * 8:tt * 8 + 1], gt[:, sl],
                                                axis=AX, negate=True))
                    v.wait_ge(cx.dve.h, m)
            @blk.scalar
            def _(s):
                for tt in range(NT):
                    sl = slice(tt * 8, (tt + 1) * 8)
                    s.wait_ge(cx.dve.h, dv0 + 2 * (tt + 1))
                    s.activation(gt[:, sl], gt[:, sl], AF.Exp,
                                 bias=ms1[:, tt * 8:tt * 8 + 1])
        with nc.Block() as blk:
            @blk.vector
            def _(v):
                for tt in range(NT):
                    sl = slice(tt * 8, (tt + 1) * 8)
                    m = cx.dve.inc(v.reduce_sum(ms2[:, tt * 8:tt * 8 + 1], gt[:, sl],
                                                axis=AX))
                    v.wait_ge(cx.dve.h, m)
                    m = cx.dve.inc(v.reciprocal(sc8[:, tt * 8:tt * 8 + 1],
                                                ms2[:, tt * 8:tt * 8 + 1]))
                    v.wait_ge(cx.dve.h, m)
                    v.tensor_scalar(gt[:, sl], gt[:, sl], sc8[:, tt * 8:tt * 8 + 1],
                                    None, op0=ALU.mult)
        with nc.Block() as blk:
            @blk.vector
            def _(v):
                for tt in range(NT):
                    sl = slice(tt * 8, (tt + 1) * 8)
                    v.memset(ms1[:, sl], -1e30)
                    v.memset(ms2[:, sl], -1e30)
                    m = cx.dve.inc(v.reduce_max(ms1[:, tt * 8:tt * 8 + 1], gt[:, sl],
                                                axis=AX))
                    v.wait_ge(cx.dve.h, m)
                    v.match_replace(out=a1[:, sl], in_to_replace=ms1[:, sl],
                                    in_values=gt[:, sl], imm_value=-1e30)
        with nc.Block() as blk:
            @blk.vector
            def _(v):
                for tt in range(NT):
                    sl = slice(tt * 8, (tt + 1) * 8)
                    m = cx.dve.inc(v.tensor_sub(oh1[:, sl], gt[:, sl], a1[:, sl]))
                    v.wait_ge(cx.dve.h, m)
                    v.tensor_scalar(oh1[:, sl], oh1[:, sl], 1e-29, 1.0,
                                    op0=ALU.mult, op1=ALU.min)
                    m = cx.dve.inc(v.reduce_max(ms2[:, tt * 8:tt * 8 + 1], a1[:, sl],
                                                axis=AX))
                    v.wait_ge(cx.dve.h, m)
                    v.match_replace(out=a2[:, sl], in_to_replace=ms2[:, sl],
                                    in_values=a1[:, sl], imm_value=-1e30)
        with nc.Block() as blk:
            @blk.vector
            def _(v):
                for tt in range(NT):
                    sl = slice(tt * 8, (tt + 1) * 8)
                    m = cx.dve.inc(v.tensor_sub(wl1[:, sl], a1[:, sl], a2[:, sl]))
                    v.wait_ge(cx.dve.h, m)
                    m = cx.dve.inc(v.tensor_scalar(wl1[:, sl], wl1[:, sl], 1e-29, 1.0,
                                                   op0=ALU.mult, op1=ALU.min))
                    v.wait_ge(cx.dve.h, m)
                    m = cx.dve.inc(v.tensor_add(wl1[:, sl], wl1[:, sl], oh1[:, sl]))
                    v.wait_ge(cx.dve.h, m)
                    v.tensor_mul(wl1[:, sl], wl1[:, sl], gt[:, sl])
        with nc.Block() as blk:
            pe0, dv0 = cx.pe.n, cx.dve.n
            @blk.tensor
            def _(te):
                for tt in range(NT):
                    if tt >= 2:
                        te.wait_ge(cx.dve.h, dv0 + (tt - 2) + 1)
                    cx.pe.inc(te.transpose(ps[tt % 2][0:8, 0:128],
                                           wl1[:, tt * 8:(tt + 1) * 8], P["idf"][:]))
            @blk.vector
            def _(v):
                for tt in range(NT):
                    v.wait_ge(cx.pe.h, pe0 + tt + 1)
                    cx.dve.inc(v.tensor_copy(wT8[0:8, tt * 128:(tt + 1) * 128],
                                             ps[tt % 2][0:8, 0:128]))
        with nc.Block() as blk:
            pe0, dv0 = cx.pe.n, cx.dve.n
            @blk.tensor
            def _(te):
                g = 0
                for tt in range(NT):
                    for hf in range(2):
                        if g >= 2:
                            te.wait_ge(cx.dve.h, dv0 + (g - 2) + 1)
                        cx.pe.inc(te.matmul(
                            ps[4 + g % 2][:], wT8[0:8, tt * 128:(tt + 1) * 128],
                            ebs[0:8, hf * 512:(hf + 1) * 512],
                            start=True, stop=True))
                        g += 1
            @blk.vector
            def _(v):
                g = 0
                for tt in range(NT):
                    for hf in range(2):
                        v.wait_ge(cx.pe.h, pe0 + g + 1)
                        xsl = x2[:, tt * DM + hf * 512:tt * DM + (hf + 1) * 512]
                        cx.dve.inc(v.tensor_add(xsl, xsl, ps[4 + g % 2][:]))
                        g += 1
        ewbf_d = P["ewbf_d"]
        ew_mk = {}
        def ew_load(sy, e_):
            b = (e_ % 2) * DC * DM
            for hc in range(DC):
                cx.dm.inc(sy.dma_start_transpose(
                    ewT[:, b + hc * DM:b + (hc + 1) * DM],
                    ewbf_d[e_ * DM:(e_ + 1) * DM, hc * 128:(hc + 1) * 128]), 16)
            ew_mk[e_] = cx.dm.n
        with nc.Block() as blk:
            @blk.sync
            def _(sy):
                if "m_ew" in P:
                    sy.wait_ge(cx.gd.h, P["m_ew"])
                ew_load(sy, 0)
        for e in range(E1):
            eb_ = (e % 2) * DC * DM
            with nc.Block() as blk:
                pe0, dv0 = cx.pe.n, cx.dve.n
                @blk.sync
                def _(sy):
                    if e + 1 < E1:
                        ew_load(sy, e + 1)
                @blk.tensor
                def _(te):
                    te.wait_ge(cx.dm.h, ew_mk[e])
                    g = 0
                    for tt in range(NT):
                        for hf in range(2):
                            if g >= 2:
                                te.wait_ge(cx.dve.h, dv0 + (g - 2) + 1)
                            mm = None
                            for hc in range(DC):
                                mm = te.matmul(
                                    ps[g % 2][:],
                                    xn1[:, hc * HALF + tt * 128:hc * HALF + (tt + 1) * 128],
                                    ewT[:, eb_ + hc * DM + hf * 512:
                                        eb_ + hc * DM + (hf + 1) * 512],
                                    start=(hc == 0), stop=(hc == DC - 1))
                            cx.pe.inc(mm)
                            g += 1
                @blk.vector
                def _(v):
                    g = 0
                    for tt in range(NT):
                        for hf in range(2):
                            v.wait_ge(cx.pe.h, pe0 + g + 1)
                            xsl = x2[:, tt * DM + hf * 512:tt * DM + (hf + 1) * 512]
                            cx.dve.inc(v.scalar_tensor_tensor(
                                out=xsl, in0=ps[g % 2][:],
                                scalar=wl1[:, tt * 8 + e:tt * 8 + e + 1],
                                in1=xsl, op0=ALU.mult, op1=ALU.add))
                            g += 1


def declare_l1_inputs(nc):
    t = {}
    t["x1r"] = nc.dram_tensor("x1r", [S_FULL, DM], F32, kind="ExternalInput").ap()
    for n in ("wq1", "wk1", "wv1", "wo1"):
        t[n] = nc.dram_tensor(n, [DM, DM], F32, kind="ExternalInput").ap()
    for n in ("bq1", "bk1", "bv1", "bo1"):
        t[n] = nc.dram_tensor(n, [DM], F32, kind="ExternalInput").ap()
    t["gw1"] = nc.dram_tensor("gw1", [E1, DM], F32, kind="ExternalInput").ap()
    t["gb1"] = nc.dram_tensor("gb1", [E1], F32, kind="ExternalInput").ap()
    t["ew1"] = nc.dram_tensor("ew1", [E1, DM, DM], F32, kind="ExternalInput").ap()
    t["eb1"] = nc.dram_tensor("eb1", [E1, DM], F32, kind="ExternalInput").ap()
    t["masku"] = nc.dram_tensor("masku", [CH, CH], F32, kind="ExternalInput").ap()
    t["is2nd"] = nc.dram_tensor("is2nd", [128, 1], F32, kind="ExternalInput").ap()
    return t


def build_l1_program():
    nc = bass.Bass("TRN2", target_bir_lowering=False, debug=False)
    es = ExitStack()
    cx = Cx(nc, es)
    t = declare_l1_inputs(nc)
    y = nc.dram_tensor("yout", [HALF, DM], F32, kind="ExternalOutput").ap()
    P = {}
    P["idf"] = cx.sb([128, 128], F32)
    P["idb"] = cx.sb([128, 128], BF16)
    P["ones_row"] = cx.sb([1, 128], F32)
    P["ones_col"] = cx.sb([128, 1], BF16)
    P["xn"] = cx.sb([128, DC * HALF], BF16)
    ps = [es.enter_context(nc.psum_tensor(f"ps{i}", [128, 512], F32)) for i in range(6)]
    pst = [es.enter_context(nc.psum_tensor("pstA", [128, 1024], BF16)),
           es.enter_context(nc.psum_tensor("pstB", [128, 1024], BF16))]
    emit_identity_ones(nc, P, cx)
    emit_l1(cx, t, P, ps, pst, fused=False)
    with nc.Block() as blk:
        @blk.sync
        def _(sy):
            for tt in range(NT):
                cx.dm.inc(sy.dma_start(y[tt * 128:(tt + 1) * 128, :],
                                       P["x2"][:, tt * DM:(tt + 1) * DM]), 16)
            sy.wait_ge(cx.dm.h, cx.dm.n)
    es.close()
    return nc


def l1_in_map(core, inputs, x1_full):
    b, par = core // 2, core % 2
    x = np.asarray(x1_full[b])
    xr = np.concatenate([x[par * HALF:(par + 1) * HALF],
                         x[(1 - par) * HALF:(2 - par) * HALF]], axis=0)
    d = {"x1r": np.ascontiguousarray(xr),
         "wq1": inputs["L1_attn_Wq"], "bq1": inputs["L1_attn_bq"],
         "wk1": inputs["L1_attn_Wk"], "bk1": inputs["L1_attn_bk"],
         "wv1": inputs["L1_attn_Wv"], "bv1": inputs["L1_attn_bv"],
         "wo1": inputs["L1_attn_Wo"], "bo1": inputs["L1_attn_bo"],
         "gw1": inputs["L1_ffn_gate_w"], "gb1": inputs["L1_ffn_gate_b"],
         "ew1": inputs["L1_ffn_expert_w"], "eb1": inputs["L1_ffn_expert_b"],
         "masku": np.triu(np.ones((CH, CH), np.float32)),
         "is2nd": np.full((128, 1), np.float32(par))}
    return {k: np.ascontiguousarray(np.asarray(v, dtype=np.float32)) for k, v in d.items()}


def emit_handshake_send(cx, t, P, Srep, hses):
    nc = cx.nc
    P["sh_state"] = nc.dram_tensor("sh_state", [2 * 1024, 65], F32,
                                   kind="Internal", addr_space="Shared").ap()
    P["sh_flag"] = nc.dram_tensor("sh_flag", [32, 4], I32, kind="Internal",
                                  addr_space="Shared").ap()
    widx = cx.sb([128, 8], I32, hses)
    ridx = cx.sb([128, 8], I32, hses)
    wfi = cx.sb([16, 1], I32, hses)
    non = cx.sb([16, 4], I32, hses)
    fl0 = cx.sb([16, 4], I32, hses)
    fl1 = cx.sb([16, 4], I32, hses)
    P["hs"] = (widx, ridx, wfi, non, fl0, fl1)
    with nc.Block() as blk:
        @blk.sync
        def _(sy):
            cx.dm.inc(sy.dma_start(widx[:], t["widx"]), 16)
            cx.dm.inc(sy.dma_start(ridx[:], t["ridx"]), 16)
            cx.dm.inc(sy.dma_start(wfi[:], t["wfidx"]), 16)
            cx.dm.inc(sy.dma_start(non[:], t["nonce"]), 16)
            sy.wait_ge(cx.dm.h, cx.dm.n)
    with nc.Block() as blk:
        @blk.gpsimd
        def _(g):
            g0 = cx.gd.n
            for c in range(DC):
                cx.gd.inc(g.indirect_dma_start(
                    out=P["sh_state"],
                    out_offset=bass.IndirectOffsetOnAxis(ap=widx[:, c:c + 1], axis=0),
                    in_=Srep[:, c * 65:(c + 1) * 65], in_offset=None), 16)
            g.wait_ge(cx.gd.h, g0 + DC * 16)
            cx.gd.inc(g.indirect_dma_start(
                out=P["sh_flag"],
                out_offset=bass.IndirectOffsetOnAxis(ap=wfi[:], axis=0),
                in_=non[:], in_offset=None), 16)
            g.wait_ge(cx.gd.h, cx.gd.n)


def emit_handshake_recv(cx, t, P, Srep):
    nc = cx.nc
    widx, ridx, wfi, non, fl0, fl1 = P["hs"]
    with nc.Block() as blk:
        @blk.gpsimd
        def _(g):
            magic = g.value_load(non[0:1, 0:1])
            with g.register("cnt") as cnt, g.register("diff") as diff:
                g.reg_mov(cnt, 0)
                g.reg_mov(diff, 1)
                with g.While(diff):
                    g.reg_add(cnt, cnt, 32)
                    g.dma_start(fl0[:], P["sh_flag"][0:16, :]).then_inc(cx.hsp.h, 16)
                    g.dma_start(fl1[:], P["sh_flag"][16:32, :]).then_inc(cx.hsp.h, 16)
                    g.wait_ge(cx.hsp.h, cnt)
                    f0 = g.value_load(fl0[0:1, 0:1])
                    f1 = g.value_load(fl1[0:1, 0:1])
                    with g.register("d0") as d0, g.register("d1") as d1:
                        g.reg_alu(d0, f0, magic, ALU.bitwise_xor)
                        g.reg_alu(d1, f1, magic, ALU.bitwise_xor)
                        g.reg_alu(diff, d0, d1, ALU.bitwise_or)
            for c in range(DC):
                cx.aux.inc(g.indirect_dma_start(
                    out=Srep[:, c * 65:(c + 1) * 65], out_offset=None,
                    in_=P["sh_state"],
                    in_offset=bass.IndirectOffsetOnAxis(ap=ridx[:, c:c + 1], axis=0)), 16)
            g.wait_ge(cx.aux.h, cx.aux.n)


def build_fused_program():
    nc = bass.Bass("TRN2", target_bir_lowering=False, debug=False)
    es = ExitStack()
    cx = Cx(nc, es)
    t = declare_l0_inputs(nc)
    t.update(declare_l1_inputs_fused(nc))
    y = nc.dram_tensor("yout", [HALF, DM], F32, kind="ExternalOutput").ap()
    P = {}
    P["idf"] = cx.sb([128, 128], F32)
    P["idb"] = cx.sb([128, 128], BF16)
    P["ones_row"] = cx.sb([1, 128], F32)
    P["ones_col"] = cx.sb([128, 1], BF16)
    P["xn"] = cx.sb([128, DC * HALF], BF16)
    P["wl0"] = cx.sb([128, NT * E0], F32)
    ps = [es.enter_context(nc.psum_tensor(f"ps{i}", [128, 512], F32)) for i in range(6)]
    pst = [es.enter_context(nc.psum_tensor("pstA", [128, 1024], BF16)),
           es.enter_context(nc.psum_tensor("pstB", [128, 1024], BF16))]
    emit_identity_ones(nc, P, cx)
    emit_l0(cx, t, P, ps, pst)
    P["x2"] = P["x1"]
    emit_l1(cx, t, P, ps, pst, fused=True)
    with nc.Block() as blk:
        @blk.sync
        def _(sy):
            for tt in range(NT):
                cx.dm.inc(sy.dma_start(y[tt * 128:(tt + 1) * 128, :],
                                       P["x2"][:, tt * DM:(tt + 1) * DM]), 16)
            sy.wait_ge(cx.dm.h, cx.dm.n)
    es.close()
    return nc


def declare_l1_inputs_fused(nc):
    t = {}
    for n in ("wq1", "wk1", "wv1", "wo1"):
        t[n] = nc.dram_tensor(n, [DM, DM], F32, kind="ExternalInput").ap()
    for n in ("bq1", "bk1", "bv1", "bo1"):
        t[n] = nc.dram_tensor(n, [DM], F32, kind="ExternalInput").ap()
    t["gw1"] = nc.dram_tensor("gw1", [E1, DM], F32, kind="ExternalInput").ap()
    t["gb1"] = nc.dram_tensor("gb1", [E1], F32, kind="ExternalInput").ap()
    t["ew1"] = nc.dram_tensor("ew1", [E1, DM, DM], F32, kind="ExternalInput").ap()
    t["eb1"] = nc.dram_tensor("eb1", [E1, DM], F32, kind="ExternalInput").ap()
    t["masku"] = nc.dram_tensor("masku", [CH, CH], F32, kind="ExternalInput").ap()
    t["is2nd"] = nc.dram_tensor("is2nd", [128, 1], F32, kind="ExternalInput").ap()
    t["widx"] = nc.dram_tensor("widx", [128, 8], I32, kind="ExternalInput").ap()
    t["ridx"] = nc.dram_tensor("ridx", [128, 8], I32, kind="ExternalInput").ap()
    t["wfidx"] = nc.dram_tensor("wfidx", [16, 1], I32, kind="ExternalInput").ap()
    t["nonce"] = nc.dram_tensor("nonce", [16, 4], I32, kind="ExternalInput").ap()
    return t


def fused_in_map(core, inputs, nonce):
    b, par = core // 2, core % 2
    d = l0_in_map(core, inputs)
    d.update({
        "wq1": inputs["L1_attn_Wq"], "bq1": inputs["L1_attn_bq"],
        "wk1": inputs["L1_attn_Wk"], "bk1": inputs["L1_attn_bk"],
        "wv1": inputs["L1_attn_Wv"], "bv1": inputs["L1_attn_bv"],
        "wo1": inputs["L1_attn_Wo"], "bo1": inputs["L1_attn_bo"],
        "gw1": inputs["L1_ffn_gate_w"], "gb1": inputs["L1_ffn_gate_b"],
        "ew1": inputs["L1_ffn_expert_w"], "eb1": inputs["L1_ffn_expert_b"],
        "masku": np.triu(np.ones((CH, CH), np.float32)),
        "is2nd": np.full((128, 1), np.float32(par)),
        "widx": (par * 1024 + np.arange(1024)).astype(np.int32).reshape(8, 128).T.copy(),
        "ridx": ((1 - par) * 1024 + np.arange(1024)).astype(np.int32).reshape(8, 128).T.copy(),
        "wfidx": (par * 16 + np.arange(16)).astype(np.int32).reshape(16, 1),
        "nonce": np.full((16, 4), nonce, np.int32),
    })
    return {k: np.ascontiguousarray(np.asarray(v, dtype=v.dtype if hasattr(v, "dtype") and np.issubdtype(np.asarray(v).dtype, np.integer) else np.float32)) for k, v in d.items()}


def kernel_fused(**inputs):
    import random
    nonce = random.randint(1, 2 ** 30)
    nc = build_fused_program()
    in_maps = [fused_in_map(c, inputs, nonce) for c in range(NC)]
    res = run_bass_kernel_spmd(nc, in_maps, core_ids=list(range(NC)))
    out = np.empty((NB, S_FULL, DM), np.float32)
    for c in range(NC):
        b, par = c // 2, c % 2
        out[b, par * HALF:(par + 1) * HALF] = res.results[c]["yout"]
    return out
